# version 06
# speedup vs baseline: 1.5097x; 1.0692x over previous
import contextlib
import numpy as np

import concourse.bass as bass
import concourse.mybir as mybir
from concourse.bass_utils import run_bass_kernel_spmd

F32 = mybir.dt.float32
BF16 = mybir.dt.bfloat16
AF = mybir.ActivationFunctionType
OP = mybir.AluOpType

B = 8
T = 4096
D = 1024
DI = 2048
DS = 16
DTR = 64
DCONV = 4
NL = 6
H = 16
DH = 64
W = 256
DFF = 2730
QK_SCALE = 8.0
EXP_BIAS = -24.0
NT = 8
TT = 512
QCH = 127
NCH_FULL = 32
QLAST = T - NCH_FULL * QCH

DEBUG_TAPS = ()
CFG = dict(n_tiles=NT, n_layers=NL, dirs=(0, 1), do_pa=True, do_pb=True,
           do_pc=True, do_pd=True, ffn_identity=False, n_chunks=NCH_FULL + 1,
           dbg_no_mask=False, dbg_no_vtr=False, dbg_no_attn=False)



def _host_consts():
    c = {}
    c["idf32"] = np.eye(128, dtype=np.float32)
    c["onescol"] = np.ones((128, 1), dtype=np.float32)
    c["onesrow"] = np.ones((1, 128), dtype=np.float32)
    bank = np.zeros(16, dtype=np.float32)
    bank[:10] = [1.0, 1.0 / D, 1e-5, -0.5, QK_SCALE, EXP_BIAS, 1e-12, 0.0, -1.0, 1e-24]
    c["bank"] = np.tile(bank[None, :], (128, 1))
    P = np.zeros((128, 128), dtype=np.float32)
    for base in (0, 64):
        for i in range(32):
            P[base + i + 32, base + i] = -1.0
            P[base + i, base + i + 32] = 1.0
    c["perm"] = P
    def mk_scan_mats(Q, fwd, ref):
        L2 = np.zeros((Q, Q + 1), dtype=np.float32)
        Mh = np.zeros((Q + 1, Q), dtype=np.float32)
        for k in range(Q):
            refin = (k <= ref) if fwd else (k >= ref)
            L2[k, Q] = -float(refin)
            for t in range(Q):
                inc = (k <= t) if fwd else (k >= t)
                L2[k, t] = float(inc) - float(refin)
                Mh[k, t] = float(inc)
        Mh[Q, :] = 1.0
        return L2, Mh
    c["L2f"], c["Mf"] = mk_scan_mats(QCH, True, 63)
    c["L2b"], c["Mb"] = mk_scan_mats(QCH, False, 63)
    c["L2fs"], c["Mfs"] = mk_scan_mats(QLAST, True, QLAST // 2 - 1)
    c["L2bs"], c["Mbs"] = mk_scan_mats(QLAST, False, QLAST // 2)
    half = np.arange(0, DH, 2, dtype=np.float64)
    inv_freq = 1.0 / 10000.0 ** (half / DH)
    t = np.arange(T, dtype=np.float64)
    freqs = t[:, None] * inv_freq[None, :]
    freqs = np.concatenate([freqs, freqs], axis=1)
    base_scale = (half + 0.4 * DH) / (1.4 * DH)
    power = (t - T // 2) / (W // 2)
    scale = base_scale[None, :] ** power[:, None]
    scale = np.concatenate([scale, scale], axis=1)
    cs_q = (np.cos(freqs) * scale).T.astype(np.float32)
    sn_q = (np.sin(freqs) * scale).T.astype(np.float32)
    cs_k = (np.cos(freqs) / scale).T.astype(np.float32)
    sn_k = (np.sin(freqs) / scale).T.astype(np.float32)
    c["tabs"] = np.ascontiguousarray(np.stack(
        [np.concatenate([x, x], axis=0) for x in (cs_q, sn_q, cs_k, sn_k)]))
    hind = np.zeros((8, 128, 16), dtype=np.float32)
    for f in range(8):
        hind[f, 0:64, 2 * f] = 1.0
        hind[f, 64:128, 2 * f + 1] = 1.0
    c["hind"] = hind
    c["hindT"] = np.ascontiguousarray(np.transpose(hind, (0, 2, 1)))
    return c


def _bank_idx():
    idx = {}
    o = 0
    for name, n in (("cw", 2 * 4 * 16), ("cb", 2 * 16), ("dtb", 2 * 16),
                    ("dp", 2 * 16), ("lng", NL * 8), ("lnb", NL * 8),
                    ("b2", NL * 8), ("b1a", NL * 22), ("b1g", NL * 22),
                    ("qsc", NL * 4)):
        idx[name] = o
        o += n
    idx["n"] = o
    return idx

BANK = _bank_idx()


def _host_weights(inp):
    w = {}
    w["w_inproj"] = np.ascontiguousarray(np.transpose(inp["L0_attn_in_proj_w"], (0, 2, 1)))
    w["w_xproj"] = np.ascontiguousarray(np.transpose(inp["L0_attn_x_proj_w"], (0, 2, 1)))
    w["w_dtproj"] = np.ascontiguousarray(np.transpose(inp["L0_attn_dt_proj_w"], (0, 2, 1)))
    w["w_outproj"] = np.ascontiguousarray(np.transpose(inp["L0_attn_out_proj_w"], (0, 2, 1)))
    w["w_qkv"] = np.ascontiguousarray(inp["L0_ffn_Wqkv"])
    w["w_o"] = np.ascontiguousarray(inp["L0_ffn_Wo"])
    w["w_1"] = np.ascontiguousarray(inp["L0_ffn_W1"])
    w["w_2"] = np.ascontiguousarray(inp["L0_ffn_W2"])
    w["alogT"] = np.ascontiguousarray(np.transpose(inp["L0_attn_A_log"], (0, 2, 1)))
    cols = []
    cw = np.transpose(inp["L0_attn_conv_w"], (0, 2, 1))
    for d in range(2):
        for k in range(4):
            cols.append(cw[d, k].reshape(16, 128).T)
    for name in ("L0_attn_conv_b", "L0_attn_dt_proj_b", "L0_attn_D_param"):
        for d in range(2):
            cols.append(inp[name][d].reshape(16, 128).T)
    for name in ("L0_ffn_ln_g", "L0_ffn_ln_b", "L0_ffn_b2"):
        for i in range(NL):
            cols.append(inp[name][i].reshape(8, 128).T)
    b1 = inp["L0_ffn_b1"]
    for i in range(NL):
        a = np.zeros((128, 22), dtype=np.float32)
        g = np.zeros((128, 22), dtype=np.float32)
        for j in range(22):
            n = min(128, DFF - j * 128)
            a[:n, j] = b1[i, j * 128: j * 128 + n]
            g[:n, j] = b1[i, DFF + j * 128: DFF + j * 128 + n]
        cols.append(a)
        cols.append(g)
    qs = inp["L0_ffn_q_scale"]
    ks = inp["L0_ffn_k_scale"]
    rotidx = (np.arange(128) % 64) ^ 32
    for i in range(NL):
        qsd = np.tile(qs[i], 2)
        ksd = np.tile(ks[i], 2)
        blk = np.stack([qsd, qsd[rotidx % 64 + (np.arange(128) // 64) * 64],
                        ksd, ksd[rotidx % 64 + (np.arange(128) // 64) * 64]], axis=1)
        cols.append(blk.astype(np.float32))
    w["smalls"] = np.ascontiguousarray(np.concatenate(cols, axis=1)).astype(np.float32)
    return w


class Prog:
    def __init__(self, cfg):
        self.cfg = cfg
        self.nc = bass.Bass(trn_type="TRN2", target_bir_lowering=False)
        self.ex = contextlib.ExitStack()
        self.sem = {}
        self.cnt = {}
        self.dram = {}
        self.sb = {}
        self.ps = {}

    def semaphore(self, name):
        h = self.ex.enter_context(self.nc.semaphore(name))
        self.sem[name] = h
        self.cnt[name] = 0
        return h

    def inc(self, name, instr, n=1):
        instr.then_inc(self.sem[name], n)
        self.cnt[name] += n
        return self.cnt[name]

    def wait(self, eng, name, val=None):
        v = self.cnt[name] if val is None else val
        if v > 0:
            eng.wait_ge(self.sem[name], v)

    def dt_in(self, name, shape, dt=F32):
        self.dram[name] = self.nc.dram_tensor(name, list(shape), dt, kind="ExternalInput").ap()
        return self.dram[name]

    def dt_out(self, name, shape, dt=F32):
        self.dram[name] = self.nc.dram_tensor(name, list(shape), dt, kind="ExternalOutput").ap()
        return self.dram[name]

    def dt_tmp(self, name, shape, dt=F32):
        self.dram[name] = self.nc.dram_tensor(name, list(shape), dt, kind="Internal").ap()
        return self.dram[name]

    def sbuf(self, name, shape, dt=F32, ex=None):
        t = (ex or self.ex).enter_context(self.nc.sbuf_tensor("sb_" + name, list(shape), dt))
        self.sb[name] = t.ap()
        return self.sb[name]

    def edge(self, eng, name, instr):
        n = self.inc(name, instr)
        eng.wait_ge(self.sem[name], n)

    def phase_barrier(self):
        sems = ("tT", "tS", "tV", "tG", "qS", "qG", "w1a", "w1b", "pcA", "pcB")
        for emit in (self.block.tensor, self.block.scalar, self.block.vector,
                     self.block.gpsimd, self.block.sync):
            def _(e):
                for s in sems:
                    self.wait(e, s)
            emit(_)

    def psum(self, name, shape, dt=F32, ex=None):
        t = (ex or self.ex).enter_context(self.nc.psum_tensor(name, list(shape), dt))
        self.ps[name] = t.ap()
        return self.ps[name]

    def bankcol(self, key, j):
        i = BANK[key] + j
        return self.sb["smalls"][:, i:i + 1]

    def const(self, which, p1=False):
        j = {"one": 0, "inv1024": 1, "lneps": 2, "neghalf": 3, "qks": 4,
             "expb": 5, "l2eps": 6, "zero": 7, "negone": 8, "l2eps2": 9}[which]
        ap = self.sb["bank"][:, j:j + 1]
        return ap[0:1, :] if p1 else ap

    def declare(self):
        self.dt_in("xT", [D, T])
        self.dt_in("w_inproj", [2, D, 2 * DI])
        self.dt_in("w_xproj", [2, DI, 96])
        self.dt_in("w_dtproj", [2, DTR, DI])
        self.dt_in("w_outproj", [2, DI, D])
        self.dt_in("w_qkv", [NL, D, 3 * D])
        self.dt_in("w_o", [NL, D, D])
        self.dt_in("w_1", [NL, D, 2 * DFF])
        self.dt_in("w_2", [NL, DFF, D])
        self.dt_in("alogT", [2, DS, DI])
        self.dt_in("smalls", [128, BANK["n"]])
        self.dt_in("hindT", [8, 16, 128])
        self.dt_in("tabs", [4, 128, T])
        self.dt_in("idf32", [128, 128])
        self.dt_in("onescol", [128, 1])
        self.dt_in("onesrow", [1, 128])
        self.dt_in("bank", [128, 16])
        self.dt_in("perm", [128, 128])
        self.dt_in("hind", [8, 128, 16])
        for nm, q in (("f", QCH), ("b", QCH), ("fs", QLAST), ("bs", QLAST)):
            self.dt_in("L2" + nm, [q, q + 1])
            self.dt_in("M" + nm, [q + 1, q])

        self.dt_out("outT", [D, T])

        self.dt_tmp("xnb_sc", [128, 8, T], BF16)
        self.dt_tmp("xfA", [128, 8, T + 2])
        self.dt_tmp("xfB", [128, 8, T + 2])
        self.dt_tmp("dtT_sc", [2, T, DI])
        self.dt_tmp("uT_sc", [2, T, DI], BF16)
        self.dt_tmp("xcb_sc", [2, 128, 16, T], BF16)
        self.dt_tmp("szf_sc", [2, 128, 16, T])
        self.dt_tmp("bct_sc", [2, T, 32])
        self.dt_tmp("yT_sc", [2, 128, 16, T])
        self.dt_tmp("yg_sc", [2, 128, 16, T], BF16)

        self.sbuf("smalls", [128, BANK["n"]])
        self.sbuf("bank", [128, 16])
        self.sbuf("idf32", [128, 128])
        self.sbuf("idbf", [128, 128], BF16)
        self.sbuf("onescol", [128, 1])
        self.sbuf("onesrow", [1, 128])
        self.sbuf("onesrow_b", [1, 128], BF16)
        self.sbuf("perm", [128, 128], BF16)
        self.sbuf("hind", [128, 8, 16], BF16)
        self.sbuf("hindT", [16, 8, 128], BF16)

        for s in ("qS", "qG", "tT", "tS", "tV", "tG", "w1a", "w1b", "pcA", "pcB"):
            self.semaphore(s)

    def p0(self):
        d = self.dram
        sb = self.sb
        def _(g):
            for src, dst in (("smalls", "smalls"), ("bank", "bank"), ("idf32", "idf32"),
                             ("idf32", "idbf"), ("onescol", "onescol"), ("onesrow", "onesrow"),
                             ("onesrow", "onesrow_b"), ("perm", "perm")):
                i = g.dma_start(out=sb[dst], in_=d[src])
                self.inc("qG", i, 16)
            i = g.dma_start(out=sb["hind"], in_=d["hind"].rearrange("f p m -> p f m"))
            self.inc("qG", i, 16)
            i = g.dma_start(out=sb["hindT"], in_=d["hindT"].rearrange("f m p -> m f p"))
            self.inc("qG", i, 16)
        self.block.gpsimd(_)
        self.p0_wait = self.cnt["qG"]

    def build(self):
        self.declare()
        with self.ex:
            self.block = self.ex.enter_context(self.nc.Block())
            self.p0()
            if self.cfg["do_pa"]:
                self.pa1()
                self.pa23()
            if self.cfg["do_pb"]:
                self.pb_main()
            if self.cfg["do_pc"]:
                self.pc()
            if self.cfg["do_pd"]:
                self.pd()
            self.emit_taps()
            self.finish()
        return self.nc

    def emit_taps(self):
        if not DEBUG_TAPS:
            return
        for name in DEBUG_TAPS:
            src = self.dram[name]
            self.dt_out("tap_" + name, list(src.shape), src.dtype)
        def _(q):
            for s in ("tT", "tS", "tV", "tG", "qG", "qS"):
                self.wait(q, s)
            for name in DEBUG_TAPS:
                i = q.dma_start(out=self.dram["tap_" + name], in_=self.dram[name])
                self.inc("qS", i, 16)
        self.block.sync(_)

    def finish(self):
        def _(q):
            for s in ("tT", "tS", "tV", "tG", "qG", "w1a", "w1b", "pcA", "pcB"):
                self.wait(q, s)
            self.wait(q, "qS")
        self.block.sync(_)


def pa1(self: Prog):
    ntile = self.cfg["n_tiles"]
    with contextlib.ExitStack() as pex:
        sb = lambda n, s, d=F32: self.sbuf(n, s, d, ex=pex)
        xs = sb("pa_xs", [128, 8, TT])
        xq = sb("pa_xq", [128, 8, TT])
        rows = sb("pa_rows", [1, 6 * TT])
        xt1 = sb("pa_xt1", [128, 8, TT])
        xn = sb("pa_xn", [128, 8, TT])
        xnb = sb("pa_xnb", [128, 8, TT], BF16)
        Q0 = self.psum("pa1_Q0", [128, 1024], ex=pex)
        Q1 = self.psum("pa1_Q1", [128, 1024], ex=pex)
        ps1 = Q0[0:1, 0:TT]
        ps2 = Q0[0:1, TT:2 * TT]
        prb = Q1[:, 0:TT]
        pmb = Q1[:, TT:2 * TT]
        xTv = self.dram["xT"].rearrange("(f p) t -> p f t", p=128)
        mu, mu2, s2s, var, rstd, mrs = (rows[:, i * TT:(i + 1) * TT] for i in range(6))

        for it in range(ntile):
            t0 = it * TT
            def _(q, t0=t0):
                self.wait(q, "qG", self.p0_wait)
                self.wait(q, "tT")
                self.wait(q, "tS")
                self.wait(q, "tV")
                i = q.dma_start(out=xs[:, :, :], in_=xTv[:, :, t0:t0 + TT])
                self.inc("qS", i, 16)
            self.block.sync(_)
            def _(t):
                self.wait(t, "qS")
                self.wait(t, "tS")
                for f in range(8):
                    i = t.matmul(ps1, self.sb["onescol"], xs[:, f, :], start=(f == 0), stop=(f == 7))
                self.inc("tT", i)
            self.block.tensor(_)
            def _(s):
                self.wait(s, "qS")
                i = s.activation(xq[:, :, :], xs[:, :, :], AF.Square)
                self.inc("tS", i)
            self.block.scalar(_)
            def _(t):
                self.wait(t, "tS")
                for f in range(8):
                    i = t.matmul(ps2, self.sb["onescol"], xq[:, f, :], start=(f == 0), stop=(f == 7))
                self.inc("tT", i)
            self.block.tensor(_)
            def _(s):
                self.wait(s, "tT")
                self.edge(s, "tS", s.activation(mu, ps1, AF.Copy, scale=self.const("inv1024", True)))
                s.activation(s2s, ps2, AF.Copy, scale=self.const("inv1024", True))
                i = s.activation(mu2, mu, AF.Square)
                self.inc("tS", i)
            self.block.scalar(_)
            def _(v):
                self.wait(v, "tS")
                i = v.tensor_tensor(var, s2s, mu2, OP.subtract)
                self.inc("tV", i)
            self.block.vector(_)
            def _(s):
                self.wait(s, "tV")
                self.edge(s, "tS", s.activation(var, var, AF.Ln, bias=self.const("lneps", True)))
                i = s.activation(rstd, var, AF.Exp, scale=self.const("neghalf", True))
                self.inc("tS", i)
            self.block.scalar(_)
            def _(v):
                self.wait(v, "tS")
                i = v.tensor_tensor(mrs, mu, rstd, OP.mult)
                self.inc("tV", i)
            self.block.vector(_)
            def _(t):
                self.wait(t, "tV")
                t.matmul(prb, self.sb["onesrow"], rstd, start=True, stop=True)
                i = t.matmul(pmb, self.sb["onesrow"], mrs, start=True, stop=True)
                self.inc("tT", i)
            self.block.tensor(_)
            def _(v):
                self.wait(v, "tT")
                for f in range(8):
                    i = v.tensor_tensor(xt1[:, f, :], xs[:, f, :], prb, OP.mult)
                self.edge(v, "tV", i)
                for f in range(8):
                    i = v.tensor_tensor(xn[:, f, :], xt1[:, f, :], pmb, OP.subtract)
                self.inc("tV", i)
            self.block.vector(_)
            def _(s):
                self.wait(s, "tV")
                i = s.copy(xnb[:, :, :], xn[:, :, :])
                self.inc("tS", i)
            self.block.scalar(_)
            def _(q, t0=t0):
                self.wait(q, "tV")
                i = q.dma_start(out=self.dram["xfA"][:, :, 1 + t0:1 + t0 + TT], in_=xn[:, :, :])
                self.inc("qS", i, 16)
                self.wait(q, "tS")
                i = q.dma_start(out=self.dram["xnb_sc"][:, :, t0:t0 + TT], in_=xnb[:, :, :])
                self.inc("qS", i, 16)
            self.block.sync(_)

        def _(v):
            self.wait(v, "tT")
            self.wait(v, "tS")
            i = v.memset(xs[:, :, 0:1], 0.0)
            self.inc("tV", i)
        self.block.vector(_)
        def _(q):
            self.wait(q, "tV")
            for nm in ("xfA", "xfB"):
                i = q.dma_start(out=self.dram[nm][:, :, 0:1], in_=xs[:, :, 0:1],
                                allow_slow_non_contiguous=True)
                self.inc("qS", i, 16)
        self.block.sync(_)
        self.phase_barrier()


TTA = 256

def pa23(self: Prog):
    ntile = T // TTA if self.cfg["n_tiles"] == NT else self.cfg["n_tiles"]
    with contextlib.ExitStack() as pex:
        sb = lambda n, s, d=F32: self.sbuf(n, s, d, ex=pex)
        w_ip = sb("pa_wip", [128, 8, 2 * DI], BF16)
        w_xp = sb("pa_wxp", [128, 16, 96], BF16)
        w_dt = sb("pa_wdt", [DTR, DI], BF16)
        dg = sb("pa_dg", [128, 4 * 16 * 128], BF16)
        xnbt = sb("pa_xnbt", [128, 8, TTA], BF16)
        xi = sb("pa_xi", [128, 16, TTA + 4], BF16)
        xihalo = sb("pa_xihalo", [128, 16, 4], BF16)
        xcb = sb("pa_xcb", [128, 16, TTA], BF16)
        szf = sb("pa_szf", [128, 16, TTA])
        dtraw = sb("pa_dtraw", [DTR, TTA], BF16)
        bc = sb("pa_bc", [32, TTA])
        bct = sb("pa_bct", [128, 2, 32])
        et = sb("pa_et", [128, TTA])
        dts = sb("pa_dts", [128, 16, TTA])
        dtb16 = sb("pa_dtb16", [128, 16, TTA], BF16)
        ub = sb("pa_ub", [128, 16, TTA], BF16)
        trf = sb("pa_trf", [128, DI])
        trb = sb("pa_trb", [128, DI], BF16)
        dgv = dg.rearrange("p (k b m) -> p k b m", k=4, b=16)
        Q0 = self.psum("pa_Q0", [128, 1024], ex=pex)
        Q1 = self.psum("pa_Q1", [128, 1024], ex=pex)
        P0 = self.psum("pa_P0", [128, 512], ex=pex)
        P1 = self.psum("pa_P1", [128, 512], ex=pex)
        PB16 = self.psum("pa_PB16", [128, 2048], BF16, ex=pex)

        for dr in self.cfg["dirs"]:
            def _(g, dr=dr):
                self.wait(g, "qG", self.p0_wait)
                self.wait(g, "tT")
                i = g.dma_start(out=w_ip[:, :, :],
                                in_=self.dram["w_inproj"][dr].rearrange("(k p) m -> p k m", p=128))
                self.inc("qG", i, 16)
                i = g.dma_start(out=w_xp[:, :, :],
                                in_=self.dram["w_xproj"][dr].rearrange("(k p) m -> p k m", p=128))
                self.inc("qG", i, 16)
                i = g.dma_start(out=w_dt[:, :], in_=self.dram["w_dtproj"][dr])
                self.inc("qG", i, 16)
            self.block.gpsimd(_)
            def _(v, dr=dr):
                self.wait(v, "qG")
                self.wait(v, "tT")
                for k in range(4):
                    for b in range(16):
                        i = v.tensor_scalar(dgv[:, k, b, :], self.sb["idbf"],
                                            self.bankcol("cw", dr * 64 + k * 16 + b), None, OP.mult)
                self.inc("tV", i)
            self.block.vector(_)
            def _(s, dr=dr):
                self.wait(s, "tV")
                i = s.memzero(xihalo[:, :, :])
                self.inc("tS", i)
            self.block.scalar(_)

            tiles = list(range(ntile)) if dr == 0 else list(range(ntile - 1, -1, -1))
            for ti, it in enumerate(tiles):
                t0 = it * TTA
                xioff = 3 if dr == 0 else 0
                def _(q, t0=t0):
                    self.wait(q, "tT")
                    self.wait(q, "tS")
                    self.wait(q, "tV")
                    i = q.dma_start(out=xnbt[:, :, :],
                                    in_=self.dram["xnb_sc"][:, :, t0:t0 + TTA])
                    self.inc("qS", i, 16)
                self.block.sync(_)

                tB = self.cnt["tT"]; sB = self.cnt["tS"]
                def _(t, tB=tB, sB=sB):
                    self.wait(t, "qS")
                    self.wait(t, "qG")
                    self.wait(t, "tV")
                    self.wait(t, "tS")
                    for mt in range(32):
                        if mt >= 2:
                            t.wait_ge(self.sem["tS"], sB + mt - 1)
                        pp = Q0[:, (mt % 2) * 512:(mt % 2) * 512 + TTA]
                        for kb in range(8):
                            i = t.matmul(pp, w_ip[:, kb, mt * 128:mt * 128 + 128], xnbt[:, kb, :],
                                         start=(kb == 0), stop=(kb == 7))
                        self.inc("tT", i)
                self.block.tensor(_)
                def _(s, dr=dr, tB=tB, xioff=xioff):
                    self.wait(s, "qS")
                    for mt in range(32):
                        s.wait_ge(self.sem["tT"], tB + mt + 1)
                        pp = Q0[:, (mt % 2) * 512:(mt % 2) * 512 + TTA]
                        if mt < 16:
                            i = s.copy(xi[:, mt, xioff:xioff + TTA], pp)
                        else:
                            i = s.activation(szf[:, mt - 16, :], pp, AF.Silu)
                        self.inc("tS", i)
                self.block.scalar(_)

                def _(v, dr=dr):
                    self.wait(v, "tS")
                    if dr == 0:
                        self.edge(v, "tV", v.tensor_copy(xi[:, :, 0:3], xihalo[:, :, 0:3]))
                        i = v.tensor_copy(xihalo[:, :, 0:3], xi[:, :, TTA:TTA + 3])
                    else:
                        self.edge(v, "tV", v.tensor_copy(xi[:, :, TTA:TTA + 3], xihalo[:, :, 0:3]))
                        i = v.tensor_copy(xihalo[:, :, 0:3], xi[:, :, 0:3])
                    self.inc("tV", i)
                self.block.vector(_)

                tB = self.cnt["tT"]; sB = self.cnt["tS"]
                def _(t, dr=dr, tB=tB, sB=sB):
                    self.wait(t, "tV")
                    for b in range(16):
                        if b >= 2:
                            t.wait_ge(self.sem["tS"], sB + b - 1)
                        pc = P0 if b % 2 == 0 else P1
                        for k in range(4):
                            off = k if dr == 0 else 3 - k
                            i = t.matmul(pc[:, 0:TTA], dgv[:, k, b, :], xi[:, b, off:off + TTA],
                                         start=(k == 0), stop=(k == 3))
                        self.inc("tT", i)
                self.block.tensor(_)
                def _(s, dr=dr, tB=tB):
                    for b in range(16):
                        s.wait_ge(self.sem["tT"], tB + b + 1)
                        pc = P0 if b % 2 == 0 else P1
                        i = s.activation(xcb[:, b, :], pc[:, 0:TTA], AF.Silu,
                                         bias=self.bankcol("cb", dr * 16 + b))
                        self.inc("tS", i)
                self.block.scalar(_)

                def _(t, dr=dr):
                    self.wait(t, "tS")
                    pxp = Q1[0:96, 0:TTA]
                    for kb in range(16):
                        i = t.matmul(pxp, w_xp[:, kb, :], xcb[:, kb, :],
                                     start=(kb == 0), stop=(kb == 15))
                    self.inc("tT", i)
                self.block.tensor(_)
                def _(s, dr=dr):
                    self.wait(s, "tT")
                    s.copy(dtraw[:, :], Q1[0:64, 0:TTA])
                    i = s.copy(bc[:, :], Q1[64:96, 0:TTA])
                    self.inc("tS", i)
                self.block.scalar(_)
                def _(t, dr=dr):
                    self.wait(t, "tS")
                    for j in range(2):
                        i = t.transpose(P0[:, j * 32:j * 32 + 32],
                                        bc[0:32, j * 128:j * 128 + 128],
                                        self.sb["idf32"][0:32, 0:32])
                    self.inc("tT", i)
                self.block.tensor(_)
                def _(s, dr=dr):
                    self.wait(s, "tT")
                    i = s.copy(bct[:, :, :], P0[:, 0:64].rearrange("p (j c) -> p j c", j=2))
                    self.inc("tS", i)
                self.block.scalar(_)
                def _(q, dr=dr, t0=t0):
                    self.wait(q, "tS")
                    i = q.dma_start(out=self.dram["bct_sc"][dr, t0:t0 + TTA, :].rearrange("(j p) c -> p j c", p=128),
                                    in_=bct[:, :, :])
                    self.inc("qS", i, 16)
                self.block.sync(_)

                tB = self.cnt["tT"]; sB = self.cnt["tS"]
                def _(t, dr=dr, tB=tB, sB=sB):
                    self.wait(t, "tS")
                    for mt in range(16):
                        if mt >= 2:
                            t.wait_ge(self.sem["tS"], sB + 3 * (mt - 1))
                        pd_ = Q1[:, (mt % 2) * 512:(mt % 2) * 512 + TTA]
                        i = t.matmul(pd_, w_dt[:, mt * 128:mt * 128 + 128], dtraw[:, :],
                                     start=True, stop=True)
                        self.inc("tT", i)
                self.block.tensor(_)
                def _(s, dr=dr, tB=tB):
                    for mt in range(16):
                        s.wait_ge(self.sem["tT"], tB + mt + 1)
                        pd_ = Q1[:, (mt % 2) * 512:(mt % 2) * 512 + TTA]
                        self.edge(s, "tS", s.activation(et[:, :], pd_, AF.Exp, bias=self.bankcol("dtb", dr * 16 + mt)))
                        self.edge(s, "tS", s.activation(dts[:, mt, :], et[:, :], AF.Ln, bias=self.const("one")))
                        i = s.copy(dtb16[:, mt, :], dts[:, mt, :])
                        self.inc("tS", i)
                self.block.scalar(_)
                def _(v, dr=dr):
                    self.wait(v, "tS")
                    i = v.tensor_tensor(ub[:, :, :], dtb16[:, :, :], xcb[:, :, :], OP.mult)
                    self.inc("tV", i)
                self.block.vector(_)
                def _(q, dr=dr, t0=t0):
                    self.wait(q, "tS")
                    i = q.dma_start(out=self.dram["xcb_sc"][dr, :, :, t0:t0 + TTA], in_=xcb[:, :, :])
                    self.inc("qS", i, 16)
                    i = q.dma_start(out=self.dram["szf_sc"][dr, :, :, t0:t0 + TTA], in_=szf[:, :, :])
                    self.inc("qS", i, 16)
                self.block.sync(_)

                for tsub in range(TTA // 128):
                    def _(t, dr=dr, tsub=tsub):
                        self.wait(t, "tS")
                        self.wait(t, "tV")
                        for dh in range(2):
                            pq = Q0 if dh == 0 else Q1
                            for dsub in range(8):
                                i = t.transpose(pq[:, dsub * 128:dsub * 128 + 128],
                                                dts[:, dh * 8 + dsub, tsub * 128:tsub * 128 + 128],
                                                self.sb["idf32"])
                            self.inc("tT", i)
                        for dd in range(16):
                            i = t.transpose(PB16[:, dd * 128:dd * 128 + 128],
                                            ub[:, dd, tsub * 128:tsub * 128 + 128],
                                            self.sb["idbf"])
                        self.inc("tT", i)
                    self.block.tensor(_)
                    def _(s, dr=dr, tsub=tsub):
                        self.wait(s, "tT")
                        self.wait(s, "qS")
                        s.copy(trf[:, 0:1024], Q0[:, :])
                        i = s.copy(trf[:, 1024:2048], Q1[:, :])
                        self.inc("tS", i)
                    self.block.scalar(_)
                    def _(v, dr=dr, tsub=tsub):
                        self.wait(v, "tT")
                        self.wait(v, "qS")
                        i = v.tensor_copy(trb[:, :], PB16[:, :])
                        self.inc("tV", i)
                    self.block.vector(_)
                    def _(q, dr=dr, t0=t0, tsub=tsub):
                        self.wait(q, "tS")
                        r0 = t0 + tsub * 128
                        i = q.dma_start(out=self.dram["dtT_sc"][dr, r0:r0 + 128, :], in_=trf[:, :])
                        self.inc("qS", i, 16)
                        self.wait(q, "tV")
                        i = q.dma_start(out=self.dram["uT_sc"][dr, r0:r0 + 128, :], in_=trb[:, :])
                        self.inc("qS", i, 16)
                    self.block.sync(_)
        self.phase_barrier()


Prog.pa1 = pa1
Prog.pa23 = pa23


def pb(self: Prog):
    ntile = self.cfg["n_tiles"]
    nlay = self.cfg["n_layers"]
    bank = self.sb["smalls"]
    for li in range(nlay):
        self.pb_attn(li, "xfA" if (li % 2 == 0) else "xfA", None)

def pb_main(self: Prog):
    ntile = self.cfg["n_tiles"]
    nlay = self.cfg["n_layers"]
    src_name = "xfA"
    dst_name = "xfB"
    for li in range(nlay):
        self.pb_attn(li, src_name, dst_name)
        src_name, dst_name = dst_name, src_name
        self.pb_ffn(li, src_name, dst_name)
        src_name, dst_name = dst_name, src_name


def pb_attn(self: Prog, li, srcn, dstn):
    ntile = self.cfg["n_tiles"]
    with contextlib.ExitStack() as pex:
        sb = lambda n, s, d=F32: self.sbuf(n + "_l%d" % li, s, d, ex=pex)
        wqkv = sb("ba_wqkv", [128, 8, 3 * D], BF16)
        wo = sb("ba_wo", [128, 8, D], BF16)
        tabs = sb("ba_tabs", [128, 2, TT], BF16)
        xf = sb("ba_xf", [128, 8, TT + 1])
        xsb = sb("ba_xsb", [128, 8, TT], BF16)
        qb = sb("ba_qb", [128, 8, TT], BF16)
        sq = sb("ba_sq", [128, 8, TT], BF16)
        t1b = sb("ba_t1", [128, 8, TT], BF16)
        t2b = sq
        vb = qb
        qh = sb("ba_qh", [128, H, TT], BF16)
        kring = sb("ba_kring", [128, H, 6, 128], BF16)
        vtr = sb("ba_vtr", [128, 6, H, 65], BF16)
        nr = sb("ba_nr", [16, TT])
        rnb16 = sb("ba_rnb", [16, TT], BF16)
        ew = sb("ba_ew", [128, 2, 4, W], BF16)
        avs = sb("ba_avs", [65, 2, W])
        rr = sb("ba_rr", [1, 2, W], BF16)
        rr32 = sb("ba_rr32", [1, 2, W])
        ao = sb("ba_ao", [128, 8, TT], BF16)
        xfp = xf
        QA = self.psum("ba_QA_l%d" % li, [128, 1024], ex=pex)
        QB = self.psum("ba_QB_l%d" % li, [128, 1024], ex=pex)
        PR = self.psum("ba_PR_l%d" % li, [128, 1024], ex=pex)
        PN = self.psum("ba_PN_l%d" % li, [128, 512], ex=pex)
        PV = self.psum("ba_PV_l%d" % li, [128, 1024], BF16, ex=pex)

        def _(g, li=li):
            self.wait(g, "tT")
            self.wait(g, "tV")
            i = g.dma_start(out=wqkv[:, :, :], in_=self.dram["w_qkv"][li].rearrange("(k p) m -> p k m", p=128))
            self.inc("qG", i, 16)
            i = g.dma_start(out=wo[:, :, :], in_=self.dram["w_o"][li].rearrange("(k p) m -> p k m", p=128))
            self.inc("qG", i, 16)
        self.block.gpsimd(_)
        def _(v, li=li):
            self.wait(v, "qG")
            v.memset(vtr[:, :, :, 64:65], 1.0)
            v.memset(qh[:, :, :], 0.0)
            i = v.memset(kring[:, :, :, :], 0.0)
            self.inc("tV", i)
        self.block.vector(_)

        for it in range(ntile):
            t0 = it * TT
            def _(q, t0=t0, srcn=srcn):
                self.wait(q, "tT")
                self.wait(q, "tS")
                self.wait(q, "tV")
                self.wait(q, "tG")
                self.wait(q, "qS")
                i = q.dma_start(out=xf[:, :, :], in_=self.dram[srcn][:, :, t0:t0 + TT + 1])
                self.inc("qS", i, 16)
            self.block.sync(_)
            def _(s):
                self.wait(s, "qS")
                for f in range(8):
                    i = s.copy(xsb[:, f, :], xf[:, f, 1:1 + TT] if f < 4 else xf[:, f, 0:TT])
                self.inc("tS", i)
            self.block.scalar(_)

            stg = self.cfg.get("dbg_stage", 9)
            if stg >= 2:
                self.pb_qk_section(li, it, t0, True, wqkv, xsb, qb, sq, t1b, t2b, None, qh, None,
                                   tabs, nr, rnb16, QA, QB, PR, PN)
            if stg >= 3:
                self.pb_qk_section(li, it, t0, False, wqkv, xsb, qb, sq, t1b, t2b, None, None, kring,
                                   tabs, nr, rnb16, QA, QB, PR, PN)
            if stg < 4:
                continue
            tB = self.cnt["tT"]
            sBv = self.cnt["tS"]
            def _(t, tB=tB, sBv=sBv):
                self.wait(t, "tS")
                self.wait(t, "tV")
                for mt in range(8):
                    if mt >= 2:
                        t.wait_ge(self.sem["tS"], sBv + mt - 1)
                    pp = QA[:, (mt % 2) * 512:(mt % 2) * 512 + 512]
                    if self.cfg.get("dbg_no_vmm"):
                        i = t.matmul(pp[:, 0:4], wqkv[:, 0, 2 * D:2 * D + 128],
                                     xsb[:, 0, 0:4], start=True, stop=True)
                    else:
                        for kb in range(8):
                            i = t.matmul(pp, wqkv[:, kb, 2 * D + mt * 128:2 * D + mt * 128 + 128],
                                         xsb[:, kb, :], start=(kb == 0), stop=(kb == 7))
                    self.inc("tT", i)
            self.block.tensor(_)
            def _(s, tB=tB):
                for mt in range(8):
                    s.wait_ge(self.sem["tT"], tB + mt + 1)
                    pp = QA[:, (mt % 2) * 512:(mt % 2) * 512 + 512]
                    if self.cfg.get("dbg_no_vevac"):
                        i = s.copy(vb[:, mt, 0:4], pp[:, 0:4])
                    else:
                        i = s.copy(vb[:, mt, :], pp)
                    self.inc("tS", i)
            self.block.scalar(_)
            tB = self.cnt["tT"]
            sB = self.cnt["tS"]
            def _(t, it=it, tB=tB, sB=sB):
                if self.cfg.get("dbg_no_vtr"):
                    for j in range(4):
                        self.inc("tT", t.nop())
                    return
                self.wait(t, "tS")
                for j in range(4):
                    if j >= 1:
                        t.wait_ge(self.sem["tS"], sB + j)
                    slot = (4 * it + j) % 6
                    for fb in range(8):
                        i = t.transpose(PV[:, fb * 128:fb * 128 + 128],
                                        vb[:, fb, j * 128:j * 128 + 128],
                                        self.sb["idbf"])
                    self.inc("tT", i)
            self.block.tensor(_)
            def _(s, it=it, tB=tB):
                for j in range(4):
                    slot = (4 * it + j) % 6
                    s.wait_ge(self.sem["tT"], tB + j + 1)
                    if self.cfg.get("dbg_no_vtrevac"):
                        i = s.copy(vtr[:, slot, 0, 0:4], PV[:, 0:4])
                    else:
                        i = s.copy(vtr[:, slot, :, 0:64], PV[:, 0:1024].rearrange("p (h c) -> p h c", h=H))
                    self.inc("tS", i)
            self.block.scalar(_)

            if stg < 5:
                continue
            seq = []
            for wloc in range(2):
                wg = 2 * it + wloc
                kb4s = [2 * wg - 2, 2 * wg - 1, 2 * wg, 2 * wg + 1]
                for h in range(H):
                    seq.append((wloc, wg, h, kb4s))
            n = len(seq)
            Tops, Sops, Gops, Vops = [], [], [], []
            for i in range(n):
                Tops.append(("sim", i))
                if i >= 1:
                    Tops.append(("av", i - 1))
                if i >= 2:
                    Tops.append(("rcpb", i - 2))
                Sops.append(("exp", i))
                if i >= 1:
                    Sops += [("evac", i - 1), ("ln", i - 1), ("rr", i - 1)]
                Gops.append(("mask", i))
                Vops.append(("div", i))
            Tops += [("av", n - 1), ("rcpb", n - 2), ("rcpb", n - 1)]
            Sops += [("evac", n - 1), ("ln", n - 1), ("rr", n - 1)]
            TC = {op: k + 1 for k, op in enumerate(Tops)}
            SC = {op: k + 1 for k, op in enumerate(Sops)}
            GC = {op: k + 1 for k, op in enumerate(Gops)}
            VC = {op: k + 1 for k, op in enumerate(Vops)}
            tB = self.cnt["tT"]; sB = self.cnt["tS"]; gB = self.cnt["tG"]; vB = self.cnt["tV"]
            simP = [QA, QB]
            avP = [PN[0:65, 0:W], PR[0:65, 0:W]]
            rcpbP = [PR[0:64, 512:512 + W], PR[0:64, 512:512 + W]]

            def _(t, seq=seq, Tops=Tops, TC=TC, SC=SC, GC=GC, VC=VC,
                  tB=tB, sB=sB, gB=gB, vB=vB):
                self.wait(t, "tV")
                self.wait(t, "tS")
                for op, i in Tops:
                    wloc, wg, h, kb4s = seq[i]
                    par = i % 2
                    q0 = wloc * W
                    valid = [k for k in range(4) if kb4s[k] >= 0]
                    if op == "sim":
                        if i >= 2:
                            t.wait_ge(self.sem["tS"], sB + SC[("exp", i - 2)])
                        for kb in valid:
                            slot = kb4s[kb] % 6
                            ii = t.matmul(simP[par][0:128, kb * 256:kb * 256 + 256],
                                          kring[:, h, slot, :], qh[:, h, q0:q0 + W],
                                          start=True, stop=True)
                        self.inc("tT", ii)
                    elif op == "av":
                        t.wait_ge(self.sem["tG"], gB + GC[("mask", i)])
                        if i >= 2:
                            t.wait_ge(self.sem["tS"], sB + SC[("evac", i - 2)])
                        first = True
                        for kb in valid:
                            slot = kb4s[kb] % 6
                            ii = t.matmul(avP[par], vtr[:, slot, h, :], ew[:, par, kb, :],
                                          start=first, stop=(kb == valid[-1]))
                            first = False
                        self.inc("tT", ii)
                    else:
                        t.wait_ge(self.sem["tS"], sB + SC[("rr", i)])
                        if i >= 1:
                            t.wait_ge(self.sem["tV"], vB + VC[("div", i - 1)])
                        ii = t.matmul(rcpbP[par], self.sb["onesrow_b"][0:1, 0:64],
                                      rr[:, par, :], start=True, stop=True)
                        self.inc("tT", ii)
            self.block.tensor(_)

            def _(s, seq=seq, Sops=Sops, TC=TC, VC=VC, tB=tB, vB=vB):
                for op, i in Sops:
                    par = i % 2
                    if op == "exp":
                        s.wait_ge(self.sem["tT"], tB + TC[("sim", i)])
                        if i >= 2:
                            s.wait_ge(self.sem["tT"], tB + TC[("av", i - 2)])
                        ii = s.activation(ew[:, par, :, :], simP[par][:, 0:1024], AF.Exp,
                                          scale=self.const("qks"), bias=self.const("expb"))
                        self.inc("tS", ii)
                    elif op == "evac":
                        s.wait_ge(self.sem["tT"], tB + TC[("av", i)])
                        if i >= 2:
                            s.wait_ge(self.sem["tV"], vB + VC[("div", i - 2)])
                        self.edge(s, "tS", s.copy(avs[:, par, :], avP[par]))
                    elif op == "ln":
                        self.edge(s, "tS", s.activation(rr32[:, par, :], avs[64:65, par, :], AF.Ln))
                    else:
                        with self.nc.allow_low_precision(reason="softmax denom bf16 operand"):
                            ii = s.activation(rr[:, par, :], rr32[:, par, :], AF.Exp,
                                              scale=self.const("negone")[0:1, :])
                        self.inc("tS", ii)
            self.block.scalar(_)

            def _(g, seq=seq, Gops=Gops, SC=SC, sB=sB):
                for op, i in Gops:
                    wloc, wg, h, kb4s = seq[i]
                    par = i % 2
                    g.wait_ge(self.sem["tS"], sB + SC[("exp", i)])
                    last = None
                    for kb in range(4):
                        if kb4s[kb] < 0:
                            continue
                        if kb < 2:
                            last = g.affine_select(ew[:, par, kb, :], ew[:, par, kb, :], [[-1, W]],
                                                   OP.is_ge, 0.0, base=kb * 128,
                                                   channel_multiplier=1)
                        else:
                            last = g.affine_select(ew[:, par, kb, :], ew[:, par, kb, :], [[1, W]],
                                                   OP.is_ge, 0.0, base=-(kb - 2) * 128,
                                                   channel_multiplier=-1)
                    self.inc("tG", last)
            self.block.gpsimd(_)

            def _(v, seq=seq, Vops=Vops, TC=TC, SC=SC, tB=tB, sB=sB):
                for op, i in Vops:
                    wloc, wg, h, kb4s = seq[i]
                    par = i % 2
                    q0 = wloc * W
                    fb = h // 2
                    pb_ = 64 * (h % 2)
                    v.wait_ge(self.sem["tT"], tB + TC[("rcpb", i)])
                    v.wait_ge(self.sem["tS"], sB + SC[("rr", i)])
                    ii = v.tensor_tensor(ao[pb_:pb_ + 64, fb, q0:q0 + W], avs[0:64, par, :],
                                         rcpbP[par][0:64, 0:W] if par == 0 else rcpbP[par][0:64, 0:W],
                                         OP.mult)
                    self.inc("tV", ii)
            self.block.vector(_)

            if stg < 6:
                continue
            tB = self.cnt["tT"]
            sB = self.cnt["tV"]
            def _(t, tB=tB, sB=sB):
                self.wait(t, "tV")
                for mt in range(8):
                    if mt >= 2:
                        t.wait_ge(self.sem["tV"], sB + mt - 1)
                    pp = QB[:, (mt % 2) * 512:(mt % 2) * 512 + 512]
                    for kb in range(8):
                        i = t.matmul(pp, wo[:, kb, mt * 128:mt * 128 + 128], ao[:, kb, :],
                                     start=(kb == 0), stop=(kb == 7))
                    self.inc("tT", i)
            self.block.tensor(_)
            def _(v, tB=tB):
                for mt in range(8):
                    v.wait_ge(self.sem["tT"], tB + mt + 1)
                    pp = QB[:, (mt % 2) * 512:(mt % 2) * 512 + 512]
                    i = v.tensor_tensor(xf[:, mt, 1:1 + TT], pp, xf[:, mt, 1:1 + TT], OP.add)
                    self.inc("tV", i)
            self.block.vector(_)
            def _(q, t0=t0, dstn=dstn):
                self.wait(q, "tV")
                i = q.dma_start(out=self.dram[dstn][:, :, 1 + t0:1 + t0 + TT], in_=xf[:, :, 1:1 + TT])
                self.inc("qS", i, 16)
            self.block.sync(_)
        self.phase_barrier()


def pb_qk_section(self: Prog, li, it, t0, is_q, wqkv, xsb, qb, sq, t1b, t2b, t3b, qh, kring,
                  tabs, nr, rnb16, QA, QB, PR, PN):
    off = 0 if is_q else D
    tj = 0 if is_q else 2
    def _(g, li=li, t0=t0, tj=tj):
        self.wait(g, "tV")
        for j in range(2):
            i = g.dma_start(out=tabs[:, j, :],
                            in_=self.dram["tabs"][tj + j, :, t0:t0 + TT])
            self.inc("qG", i, 16)
    self.block.gpsimd(_)
    def _(v, li=li, tj=tj):
        self.wait(v, "qG")
        for j in range(2):
            i = v.tensor_scalar(tabs[:, j, :], tabs[:, j, :],
                                self.bankcol("qsc", li * 4 + tj + j), None, OP.mult)
        self.inc("tV", i)
    self.block.vector(_)
    tB = self.cnt["tT"]
    sB = self.cnt["tS"]
    def _(t, tB=tB, sB=sB):
        self.wait(t, "tS")
        self.wait(t, "tV")
        for mt in range(8):
            if mt >= 2:
                t.wait_ge(self.sem["tS"], sB + 2 * (mt - 1))
            pp = QA[:, (mt % 2) * 512:(mt % 2) * 512 + 512]
            for kb in range(8):
                i = t.matmul(pp, wqkv[:, kb, off + mt * 128:off + mt * 128 + 128],
                             xsb[:, kb, :], start=(kb == 0), stop=(kb == 7))
            self.inc("tT", i)
    self.block.tensor(_)
    def _(s, tB=tB):
        for mt in range(8):
            s.wait_ge(self.sem["tT"], tB + mt + 1)
            pp = QA[:, (mt % 2) * 512:(mt % 2) * 512 + 512]
            i = s.activation(sq[:, mt, :], pp, AF.Square)
            self.inc("tS", i)
            i = s.copy(qb[:, mt, :], pp)
            self.inc("tS", i)
    self.block.scalar(_)
    def _(t):
        self.wait(t, "tS")
        for f in range(8):
            i = t.matmul(PN[0:16, :], self.sb["hind"][:, f, :], sq[:, f, :],
                         start=(f == 0), stop=(f == 7))
        self.inc("tT", i)
    self.block.tensor(_)
    def _(v):
        self.wait(v, "tT")
        i = v.tensor_scalar(nr[:, :], PN[0:16, :], self.const("l2eps2")[0:16, :], None, OP.max)
        self.inc("tV", i)
    self.block.vector(_)
    def _(s):
        self.wait(s, "tV")
        self.edge(s, "tS", s.activation(nr[:, :], nr[:, :], AF.Ln))
        i = s.activation(rnb16[:, :], nr[:, :], AF.Exp, scale=self.const("neghalf")[0:16, :])
        self.inc("tS", i)
    self.block.scalar(_)
    tB = self.cnt["tT"]
    vB = self.cnt["tV"]
    def _(t, tB=tB, vB=vB):
        self.wait(t, "tV")
        self.wait(t, "tS")
        for f in range(8):
            if f >= 2:
                t.wait_ge(self.sem["tV"], vB + 4 * f - 4)
            i = t.matmul(PR[:, (f % 2) * 512:(f % 2) * 512 + 512],
                         self.sb["hindT"][:, f, :], rnb16[:, :], start=True, stop=True)
            self.inc("tT", i)
            i = t.matmul(QB[:, (f % 2) * 512:(f % 2) * 512 + 512],
                         self.sb["perm"], qb[:, f, :], start=True, stop=True)
            self.inc("tT", i)
    self.block.tensor(_)
    def _(v, it=it, t0=t0, tB=tB, is_q=is_q):
        for f in range(8):
            i = v.tensor_tensor(t1b[:, f, :], qb[:, f, :], tabs[:, 0, :], OP.mult)
            self.inc("tV", i)
            v.wait_ge(self.sem["tT"], tB + 2 * f + 2)
            i = v.tensor_tensor(t2b[:, f, :], QB[:, (f % 2) * 512:(f % 2) * 512 + 512],
                                tabs[:, 1, :], OP.mult)
            self.edge(v, "tV", i)
            i = v.tensor_tensor(t1b[:, f, :], t1b[:, f, :], t2b[:, f, :], OP.add)
            self.edge(v, "tV", i)
            if is_q:
                for hh in range(2):
                    b0 = 64 * hh
                    i = v.tensor_tensor(qh[b0:b0 + 64, 2 * f + hh, :], t1b[b0:b0 + 64, f, :],
                                        PR[b0:b0 + 64, (f % 2) * 512:(f % 2) * 512 + 512], OP.mult)
                self.inc("tV", i)
            else:
                for j in range(4):
                    slot = (4 * it + j) % 6
                    for hh in range(2):
                        b0 = 64 * hh
                        i = v.tensor_tensor(kring[b0:b0 + 64, 2 * f + hh, slot, :],
                                            t1b[b0:b0 + 64, f, j * 128:j * 128 + 128],
                                            PR[b0:b0 + 64, (f % 2) * 512 + j * 128:(f % 2) * 512 + j * 128 + 128],
                                            OP.mult)
                self.inc("tV", i)
    self.block.vector(_)


def pb_ffn(self: Prog, li, srcn, dstn):
    ntile = self.cfg["n_tiles"]
    with contextlib.ExitStack() as pex:
        sb = lambda n, s, d=F32: self.sbuf(n + "_l%d" % li, s, d, ex=pex)
        w2s = sb("bf_w2", [128, 22, D], BF16)
        w1s = sb("bf_w1", [128, 2, 8, 256], BF16)
        xf = sb("bf_xf", [128, 8, TT + 1])
        xsq = sb("bf_xsq", [128, 8, TT + 1])
        rows = sb("bf_rows", [1, 6 * TT])
        xt1 = sb("bf_xt1", [128, 8, TT])
        xnbt = sb("bf_xnb", [128, 8, TT], BF16)
        gel = sb("bf_gel", [128, 2, TT])
        hbuf = sb("bf_h", [128, 22, TT], BF16)
        xfp = sb("bf_xfp", [128, 8, TT])
        QA = self.psum("bf_QA_l%d" % li, [128, 1024], ex=pex)
        QB = self.psum("bf_QB_l%d" % li, [128, 1024], ex=pex)
        PR = self.psum("bf_PR_l%d" % li, [128, 1024], ex=pex)
        PS = self.psum("bf_PS_l%d" % li, [128, 512], ex=pex)
        mu, mu2, s2s, var, rstd, mrs = (rows[:, i * TT:(i + 1) * TT] for i in range(6))
        ps1 = PS[0:1, 0:TT]
        ps2 = PS[0:1, TT - 1:TT - 1] if False else PS[0:1, 0:TT]
        ps1 = PR[0:1, 0:TT]
        ps2 = PR[0:1, TT:2 * TT]
        rbp = QB[:, 0:TT]
        mbp = QB[:, TT:2 * TT]

        def _(g, li=li):
            self.wait(g, "tT")
            i = g.dma_start(out=w2s[:, 0:21, :],
                            in_=self.dram["w_2"][li, 0:2688, :].rearrange("(k p) m -> p k m", p=128))
            self.inc("qG", i, 16)
            i = g.dma_start(out=w2s[0:42, 21, :], in_=self.dram["w_2"][li, 2688:2730, :])
            self.inc("qG", i, 16)
        self.block.gpsimd(_)

        for it in range(ntile):
            t0 = it * TT
            def _(q, t0=t0, srcn=srcn):
                self.wait(q, "tT")
                self.wait(q, "tS")
                self.wait(q, "tV")
                i = q.dma_start(out=xf[:, :, :], in_=self.dram[srcn][:, :, t0:t0 + TT + 1])
                self.inc("qS", i, 16)
            self.block.sync(_)
            def _(s):
                self.wait(s, "qS")
                i = s.activation(xsq[:, :, :], xf[:, :, :], AF.Square)
                self.inc("tS", i)
            self.block.scalar(_)
            def _(t):
                self.wait(t, "qS")
                self.wait(t, "tV")
                for f in range(8):
                    sl = (slice(1, 1 + TT) if f < 4 else slice(0, TT))
                    i = t.matmul(ps1, self.sb["onescol"], xf[:, f, sl], start=(f == 0), stop=(f == 7))
                self.inc("tT", i)
            self.block.tensor(_)
            def _(t):
                self.wait(t, "tS")
                for f in range(8):
                    sl = (slice(1, 1 + TT) if f < 4 else slice(0, TT))
                    i = t.matmul(ps2, self.sb["onescol"], xsq[:, f, sl], start=(f == 0), stop=(f == 7))
                self.inc("tT", i)
            self.block.tensor(_)
            def _(s):
                self.wait(s, "tT")
                self.edge(s, "tS", s.activation(mu, ps1, AF.Copy, scale=self.const("inv1024", True)))
                s.activation(s2s, ps2, AF.Copy, scale=self.const("inv1024", True))
                i = s.activation(mu2, mu, AF.Square)
                self.inc("tS", i)
            self.block.scalar(_)
            def _(v):
                self.wait(v, "tS")
                i = v.tensor_tensor(var, s2s, mu2, OP.subtract)
                self.inc("tV", i)
            self.block.vector(_)
            def _(s):
                self.wait(s, "tV")
                self.edge(s, "tS", s.activation(var, var, AF.Ln, bias=self.const("lneps", True)))
                i = s.activation(rstd, var, AF.Exp, scale=self.const("neghalf", True))
                self.inc("tS", i)
            self.block.scalar(_)
            def _(v):
                self.wait(v, "tS")
                i = v.tensor_tensor(mrs, mu, rstd, OP.mult)
                self.inc("tV", i)
            self.block.vector(_)
            def _(t):
                self.wait(t, "tV")
                t.matmul(rbp, self.sb["onesrow"], rstd, start=True, stop=True)
                i = t.matmul(mbp, self.sb["onesrow"], mrs, start=True, stop=True)
                self.inc("tT", i)
            self.block.tensor(_)
            def _(v):
                self.wait(v, "tT")
                for f in range(8):
                    sl = (slice(1, 1 + TT) if f < 4 else slice(0, TT))
                    i = v.tensor_tensor(xt1[:, f, :], xf[:, f, sl], rbp, OP.mult)
                self.edge(v, "tV", i)
                for f in range(8):
                    i = v.tensor_tensor(xt1[:, f, :], xt1[:, f, :], mbp, OP.subtract)
                self.inc("tV", i)
            self.block.vector(_)
            def _(v, li=li):
                self.wait(v, "tV")
                for f in range(8):
                    i = v.tensor_scalar(xnbt[:, f, :], xt1[:, f, :],
                                        self.bankcol("lng", li * 8 + f),
                                        self.bankcol("lnb", li * 8 + f), OP.mult, OP.add)
                    self.inc("tV", i)
            self.block.vector(_)

            gB = self.cnt["qG"]
            tB = self.cnt["tT"]
            sB = self.cnt["tS"]
            vB = self.cnt["tV"]
            w1aB = self.cnt["w1a"]
            w1bB = self.cnt["w1b"]
            w1v = self.dram["w_1"][li].rearrange("(k p) m -> p k m", p=128)
            def _(g, li=li, gB=gB, tB=tB):
                self.wait(g, "tT")
                self.wait(g, "tS")
                self.wait(g, "tV")
                for j in range(22):
                    na = min(128, DFF - j * 128)
                    if j >= 2:
                        g.wait_ge(self.sem["tT"], tB + 2 * (j - 1))
                    wsem = "w1a" if j % 2 == 0 else "w1b"
                    i = g.dma_start(out=w1s[:, j % 2, :, 0:na],
                                    in_=w1v[:, :, j * 128:j * 128 + na])
                    self.inc(wsem, i, 16)
                    i = g.dma_start(out=w1s[:, j % 2, :, 128:128 + na],
                                    in_=w1v[:, :, DFF + j * 128:DFF + j * 128 + na])
                    self.inc(wsem, i, 16)
            self.block.gpsimd(_)
            def _(t, gB=gB, vB=vB, w1aB=w1aB, w1bB=w1bB):
                self.wait(t, "tS")
                self.wait(t, "tV")
                for j in range(22):
                    na = min(128, DFF - j * 128)
                    PPX = QA if j % 2 == 0 else QB
                    if j % 2 == 0:
                        t.wait_ge(self.sem["w1a"], w1aB + 32 * (j // 2 + 1))
                    else:
                        t.wait_ge(self.sem["w1b"], w1bB + 32 * (j // 2 + 1))
                    if j >= 2:
                        t.wait_ge(self.sem["tV"], vB + j - 1)
                    for kb in range(8):
                        i = t.matmul(PPX[0:na, 0:512], w1s[:, j % 2, kb, 0:na], xnbt[:, kb, :],
                                     start=(kb == 0), stop=(kb == 7))
                    self.inc("tT", i)
                    for kb in range(8):
                        i = t.matmul(PPX[0:na, 512:1024], w1s[:, j % 2, kb, 128:128 + na], xnbt[:, kb, :],
                                     start=(kb == 0), stop=(kb == 7))
                    self.inc("tT", i)
            self.block.tensor(_)
            def _(s, li=li, tB=tB, vB=vB):
                for j in range(22):
                    na = min(128, DFF - j * 128)
                    PPX = QA if j % 2 == 0 else QB
                    s.wait_ge(self.sem["tT"], tB + 2 * (j + 1))
                    if j >= 2:
                        s.wait_ge(self.sem["tV"], vB + j - 1)
                    i = s.activation(gel[0:na, j % 2, :], PPX[0:na, 512:1024], AF.Gelu,
                                     bias=self.bankcol("b1g", li * 22 + j)[0:na, :])
                    self.inc("tS", i)
            self.block.scalar(_)
            def _(v, li=li, tB=tB, sB=sB):
                for j in range(22):
                    na = min(128, DFF - j * 128)
                    PPX = QA if j % 2 == 0 else QB
                    v.wait_ge(self.sem["tS"], sB + j + 1)
                    i = v.scalar_tensor_tensor(hbuf[0:na, j, :], PPX[0:na, 0:512],
                                               self.bankcol("b1a", li * 22 + j)[0:na, :],
                                               gel[0:na, j % 2, :], OP.add, OP.mult)
                    self.inc("tV", i)
            self.block.vector(_)

            tB = self.cnt["tT"]
            vB = self.cnt["tV"]
            def _(t, tB=tB, vB=vB):
                self.wait(t, "tV")
                self.wait(t, "qG")
                for mt in range(8):
                    if mt >= 2:
                        t.wait_ge(self.sem["tV"], vB + mt - 1)
                    pp = QB[:, (mt % 2) * 512:(mt % 2) * 512 + 512]
                    for kb in range(22):
                        na = min(128, DFF - kb * 128)
                        i = t.matmul(pp, w2s[0:na, kb, mt * 128:mt * 128 + 128], hbuf[0:na, kb, :],
                                     start=(kb == 0), stop=(kb == 21))
                    self.inc("tT", i)
            self.block.tensor(_)
            def _(v, li=li, tB=tB):
                for mt in range(8):
                    v.wait_ge(self.sem["tT"], tB + mt + 1)
                    pp = QB[:, (mt % 2) * 512:(mt % 2) * 512 + 512]
                    i = v.scalar_tensor_tensor(xfp[:, mt, :], pp, self.bankcol("b2", li * 8 + mt),
                                               xf[:, mt, 1:1 + TT], OP.add, OP.add)
                    self.inc("tV", i)
            self.block.vector(_)
            def _(q, t0=t0, dstn=dstn):
                self.wait(q, "tV")
                i = q.dma_start(out=self.dram[dstn][:, :, 1 + t0:1 + t0 + TT], in_=xfp[:, :, :])
                self.inc("qS", i, 16)
            self.block.sync(_)
        self.phase_barrier()

Prog.pb_attn = pb_attn
Prog.pb_ffn = pb_ffn
Prog.pb_qk_section = pb_qk_section

def pc(self: Prog):
    nch = self.cfg["n_chunks"]
    with contextlib.ExitStack() as pex:
        sb = lambda n, s, d=F32: self.sbuf(n, s, d, ex=pex)
        l2m = {}
        for nm, q in (("f", QCH), ("b", QCH), ("fs", QLAST), ("bs", QLAST)):
            l2m["L2" + nm] = sb("pc_L2" + nm, [q, q + 1])
            l2m["M" + nm] = sb("pc_M" + nm, [q + 1, q], BF16)
        negAT = sb("pc_negAT", [DS, DI], BF16)
        arow = sb("pc_arow", [1, 1024])
        negab = sb("pc_negab", [128, DS, 1024], BF16)
        wbuf = sb("pc_wbuf", [128, DS, 1024], BF16)
        rp = sb("pc_rp", [128, DS, 1024], BF16)
        hbuf = sb("pc_hbuf", [128, DS, 1024], BF16)
        etmp = sb("pc_etmp", [128, 2, 1024])
        rmt = sb("pc_rmt", [128, 2, 1024], BF16)
        cdq = sb("pc_cdq", [128, 1024])
        dtt = sb("pc_dtt", [128, 2, 1024])
        utt = sb("pc_utt", [128, 2, 1024], BF16)
        bctt = sb("pc_bctt", [128, 2, 32])
        bcttb = sb("pc_bcttb", [128, 2, 16], BF16)
        mstage = sb("pc_mstage", [1, 1024], BF16)
        ea = sb("pc_ea", [DS, 1024])
        gg = sb("pc_gg", [DS, 1024], BF16)
        wc = sb("pc_wc", [DS, 1024], BF16)
        hprev = sb("pc_hprev", [DS, 1024], BF16)
        ys = sb("pc_ys", [128, 1024])
        yts = sb("pc_yts", [128, 8, QCH])
        QA = self.psum("pc_QA", [128, 1024], ex=pex)
        QB = self.psum("pc_QB", [128, 1024], ex=pex)
        QY = self.psum("pc_QY", [128, 1024], ex=pex)
        MC = self.psum("pc_MC", [128, 1024], ex=pex)
        MB = MC[0:DS, :]

        def _(v):
            self.wait(v, "tT")
            self.wait(v, "tS")
            v.memset(dtt[:, :], 0.0)
            v.memset(utt[:, :], 0.0)
            v.memset(bctt[:, :], 0.0)
            v.memset(bcttb[:, :], 0.0)
            i = v.memset(cdq[:, :], 0.0)
            self.inc("tV", i)
        self.block.vector(_)
        def _(g):
            self.wait(g, "tT")
            self.wait(g, "tS")
            self.wait(g, "tV")
            for nm in ("L2f", "L2b", "L2fs", "L2bs"):
                i = g.dma_start(out=l2m[nm], in_=self.dram[nm])
                self.inc("qG", i, 16)
            for nm in ("Mf", "Mb", "Mfs", "Mbs"):
                i = g.dma_start(out=l2m[nm], in_=self.dram[nm])
                self.inc("qG", i, 16)
        self.block.gpsimd(_)

        marks = {"prev": None}
        for dr in self.cfg["dirs"]:
            for hh in range(2):
                def _(g, dr=dr, hh=hh):
                    self.wait(g, "tS")
                    self.wait(g, "tV")
                    i = g.dma_start(out=ea[:, :], in_=self.dram["alogT"][dr, :, hh * 1024:hh * 1024 + 1024])
                    self.inc("qG", i, 16)
                self.block.gpsimd(_)
                def _(s, dr=dr, hh=hh):
                    self.wait(s, "qG")
                    i = s.activation(negAT[:, hh * 1024:hh * 1024 + 1024], ea[:, :], AF.Exp)
                    self.inc("tS", i)
                self.block.scalar(_)

            for hf in range(2):
                d0 = hf * 1024
                for s_ in range(DS):
                    def _(g, dr=dr, s_=s_, d0=d0):
                        self.wait(g, "tS")
                        self.wait(g, "tV")
                        i = g.dma_start(out=arow[:, :], in_=self.dram["alogT"][dr, s_:s_ + 1, d0:d0 + 1024])
                        self.inc("qG", i, 16)
                    self.block.gpsimd(_)
                    def _(s, s_=s_):
                        self.wait(s, "qG")
                        i = s.activation(mstage[:, :], arow[:, :], AF.Exp)
                        self.inc("tS", i)
                    self.block.scalar(_)
                    def _(t, s_=s_):
                        self.wait(t, "tS")
                        t.matmul(QA[:, 0:512], self.sb["onesrow_b"][0:1, :], mstage[:, 0:512], start=True, stop=True)
                        i = t.matmul(QA[:, 512:1024], self.sb["onesrow_b"][0:1, :], mstage[:, 512:1024], start=True, stop=True)
                        self.inc("tT", i)
                    self.block.tensor(_)
                    def _(s, s_=s_):
                        self.wait(s, "tT")
                        i = s.copy(negab[:, s_, :], QA)
                        self.inc("tS", i)
                    self.block.scalar(_)
                def _(v, dr=dr, hf=hf):
                    self.wait(v, "tS")
                    self.wait(v, "qG")
                    i = v.memset(hprev[:, :], 0.0)
                    self.inc("tV", i)
                self.block.vector(_)

                if dr == 0:
                    chunks = [(c, QCH) for c in range(nch - 1)] + [(nch - 1, QLAST)]
                    mats = ("L2f", "Mf", "L2fs", "Mfs")
                else:
                    chunks = [(nch - 1, QLAST)] + [(c, QCH) for c in range(nch - 2, -1, -1)]
                    mats = ("L2b", "Mb", "L2bs", "Mbs")

                marks["prev"] = None
                nch_l = len(chunks)
                for ci, (c, Q) in enumerate(chunks):
                    c0 = c * QCH
                    short = (Q != QCH)
                    L2 = l2m[mats[2] if short else mats[0]]
                    Mh = l2m[mats[3] if short else mats[1]]
                    rowe = (Q - 1) if dr == 0 else 0
                    cp = ci % 2

                    def _(q, dr=dr, c0=c0, Q=Q, d0=d0, ci=ci, cp=cp):
                        if ci >= 2:
                            m = marks[ci - 2]
                            q.wait_ge(self.sem["tT"], m[0])
                            q.wait_ge(self.sem["tS"], m[1])
                            q.wait_ge(self.sem["tV"], m[2])
                        else:
                            self.wait(q, "tT")
                            self.wait(q, "tV")
                            self.wait(q, "tS")
                        psem = "pcA" if cp == 0 else "pcB"
                        i = q.dma_start(out=dtt[0:Q, cp, :], in_=self.dram["dtT_sc"][dr, c0:c0 + Q, d0:d0 + 1024])
                        self.inc(psem, i, 16)
                        i = q.dma_start(out=utt[0:Q, cp, :], in_=self.dram["uT_sc"][dr, c0:c0 + Q, d0:d0 + 1024])
                        self.inc(psem, i, 16)
                        i = q.dma_start(out=bctt[0:Q, cp, :], in_=self.dram["bct_sc"][dr, c0:c0 + Q, :])
                        self.inc(psem, i, 16)
                    self.block.sync(_)
                    psem = "pcA" if cp == 0 else "pcB"
                    qs_need = self.cnt[psem]
                    prev = marks.get("prev")
                    Vord = [("EA", -1), ("E", 0), ("E", 1), ("stt", 0), ("E", 2), ("stt", 1), ("wc", -1)]
                    for s_ in range(3, DS):
                        Vord += [("Hp", s_ - 3), ("E", s_), ("stt", s_ - 1)]
                    Vord += [("Hp", DS - 3), ("stt", DS - 1), ("Hp", DS - 2), ("Hp", DS - 1)]
                    VP = {op: k + 1 for k, op in enumerate(Vord)}

                    def _(t, Q=Q, L2=L2, cp=cp, qs_need=qs_need, psem=psem, prev=prev):
                        t.wait_ge(self.sem[psem], qs_need)
                        if prev:
                            t.wait_ge(self.sem["tS"], prev["sEvac"])
                            t.wait_ge(self.sem["tV"], prev["vB"] + 1)
                        else:
                            self.wait(t, "tS")
                            self.wait(t, "tV")
                        t.matmul(MC[0:Q + 1, 0:512], L2[0:Q, :], dtt[0:Q, cp, 0:512], start=True, stop=True)
                        i = t.matmul(MC[0:Q + 1, 512:1024], L2[0:Q, :], dtt[0:Q, cp, 512:1024], start=True, stop=True)
                        self.inc("tT", i)
                    self.block.tensor(_)
                    t1pos = self.cnt["tT"]
                    def _(s, Q=Q, cp=cp, qs_need=qs_need, psem=psem, prev=prev, t1pos=t1pos):
                        s.wait_ge(self.sem["tT"], t1pos)
                        if prev:
                            s.wait_ge(self.sem["tV"], prev["vB"] + prev["VP"][("E", DS - 1)])
                            s.wait_ge(self.sem["qG"], prev["qg_m"])
                        else:
                            self.wait(s, "tV")
                            self.wait(s, "qG")
                        i = s.copy(cdq[0:Q + 1, :], MC[0:Q + 1, :])
                        self.inc("tS", i)
                        s.wait_ge(self.sem[psem], qs_need)
                        i = s.copy(bcttb[0:Q, cp, :], bctt[0:Q, cp, 0:16])
                        self.inc("tS", i)
                    self.block.scalar(_)
                    sEvac = self.cnt["tS"] - 1
                    def _(g, Q=Q, sEvac=sEvac):
                        g.wait_ge(self.sem["tS"], sEvac)
                        i = g.dma_start(out=mstage[:, :], in_=cdq[Q:Q + 1, :])
                        self.inc("qG", i, 16)
                    self.block.gpsimd(_)
                    qg_m = self.cnt["qG"]
                    def _(t, qg_m=qg_m, sEvac=sEvac):
                        t.wait_ge(self.sem["qG"], qg_m)
                        t.wait_ge(self.sem["tS"], sEvac)
                        t.matmul(MB[:, 0:512], self.sb["onesrow_b"][0:1, 0:DS], mstage[:, 0:512], start=True, stop=True)
                        i = t.matmul(MB[:, 512:1024], self.sb["onesrow_b"][0:1, 0:DS], mstage[:, 512:1024], start=True, stop=True)
                        self.inc("tT", i)
                    self.block.tensor(_)
                    t2pos = self.cnt["tT"]
                    vB = self.cnt["tV"]
                    def _(s, vB=vB):
                        s.wait_ge(self.sem["tV"], vB + 1)
                        i = s.activation(gg[:, :], ea[:, :], AF.Exp)
                        self.inc("tS", i)
                    self.block.scalar(_)
                    sB = self.cnt["tS"]
                    def _(g, Q=Q, vB=vB, VP=VP):
                        g.wait_ge(self.sem["tV"], vB + VP[("wc", -1)])
                        i = g.dma_start(out=wbuf[Q:Q + 1, :, :], in_=wc[:, :])
                        self.inc("qG", i, 16)
                    self.block.gpsimd(_)
                    qg_c = self.cnt["qG"]

                    tB3 = self.cnt["tT"]
                    tpos_P = {}
                    tk = 0
                    for s_ in range(DS):
                        tk += 1
                        tpos_P[s_] = tB3 + tk
                        if s_ >= 1:
                            tk += 1
                    tk += 1
                    t3end = tB3 + tk

                    def _(v, Q=Q, cp=cp, d0=d0, vB=vB, sB=sB, Vord=Vord, tpos_P=tpos_P, t2pos=t2pos):
                        for op, s_ in Vord:
                            par = s_ % 2
                            if op == "EA":
                                v.wait_ge(self.sem["tT"], t2pos)
                                i = v.tensor_tensor(ea[:, :], negAT[:, d0:d0 + 1024], MB, OP.mult)
                            elif op == "wc":
                                v.wait_ge(self.sem["tS"], sB)
                                i = v.tensor_tensor(wc[:, :], hprev[:, :], gg[:, :], OP.mult)
                            elif op == "E":
                                if s_ >= 2:
                                    v.wait_ge(self.sem["tS"], sB + 2 * (s_ - 1))
                                i = v.tensor_tensor(etmp[:, par, :], cdq[:, :], negab[:, s_, :], OP.mult)
                            elif op == "stt":
                                v.wait_ge(self.sem["tS"], sB + 2 * s_ + 1)
                                i = v.scalar_tensor_tensor(wbuf[:, s_, :], rmt[:, par, :],
                                                           bcttb[:, cp, s_:s_ + 1], utt[:, cp, :], OP.mult, OP.mult)
                            else:
                                v.wait_ge(self.sem["tT"], tpos_P[s_])
                                if s_ == DS - 1:
                                    v.wait_ge(self.sem["tS"], sB + 2 * s_ + 2)
                                pp = QA if s_ % 2 == 0 else QB
                                i = v.scalar_tensor_tensor(hbuf[0:Q, s_, :], pp[0:Q, :],
                                                           bctt[0:Q, cp, 16 + s_:17 + s_], rp[0:Q, s_, :],
                                                           OP.mult, OP.mult)
                            self.inc("tV", i)
                    self.block.vector(_)
                    def _(s, vB=vB, VP=VP):
                        for s_ in range(DS):
                            par = s_ % 2
                            s.wait_ge(self.sem["tV"], vB + VP[("E", s_)])
                            if s_ >= 2:
                                s.wait_ge(self.sem["tV"], vB + VP[("stt", s_ - 2)])
                            i = s.activation(rmt[:, par, :], etmp[:, par, :], AF.Exp)
                            self.inc("tS", i)
                            i = s.activation(rp[:, s_, :], etmp[:, par, :], AF.Exp, scale=self.const("negone"))
                            self.inc("tS", i)
                    self.block.scalar(_)
                    def _(t, Q=Q, Mh=Mh, vB=vB, VP=VP, qg_c=qg_c, prev=prev):
                        t.wait_ge(self.sem["qG"], qg_c)
                        if prev:
                            t.wait_ge(self.sem["tS"], prev["s5end"])
                        for s_ in range(DS):
                            t.wait_ge(self.sem["tV"], vB + VP[("stt", s_)])
                            if s_ >= 2:
                                t.wait_ge(self.sem["tV"], vB + VP[("Hp", s_ - 2)])
                            pp = QA if s_ % 2 == 0 else QB
                            t.matmul(pp[0:Q, 0:512], Mh[0:Q + 1, 0:Q], wbuf[0:Q + 1, s_, 0:512],
                                     start=True, stop=True)
                            i = t.matmul(pp[0:Q, 512:1024], Mh[0:Q + 1, 0:Q], wbuf[0:Q + 1, s_, 512:1024],
                                         start=True, stop=True)
                            self.inc("tT", i)
                            if s_ >= 1:
                                t.wait_ge(self.sem["tV"], vB + VP[("Hp", s_ - 1)])
                                t.matmul(QY[0:Q, 0:512], self.sb["idbf"][0:Q, 0:Q], hbuf[0:Q, s_ - 1, 0:512],
                                         start=(s_ == 1), stop=False)
                                i = t.matmul(QY[0:Q, 512:1024], self.sb["idbf"][0:Q, 0:Q], hbuf[0:Q, s_ - 1, 512:1024],
                                             start=(s_ == 1), stop=False)
                                self.inc("tT", i)
                        t.wait_ge(self.sem["tV"], vB + VP[("Hp", DS - 1)])
                        t.matmul(QY[0:Q, 0:512], self.sb["idbf"][0:Q, 0:Q], hbuf[0:Q, DS - 1, 0:512],
                                 start=False, stop=True)
                        i = t.matmul(QY[0:Q, 512:1024], self.sb["idbf"][0:Q, 0:Q], hbuf[0:Q, DS - 1, 512:1024],
                                     start=False, stop=True)
                        self.inc("tT", i)
                    self.block.tensor(_)
                    assert self.cnt["tT"] == t3end, (self.cnt["tT"], t3end)
                    def _(g, rowe=rowe, vB=vB, VP=VP):
                        g.wait_ge(self.sem["tV"], vB + VP[("Hp", DS - 1)])
                        i = g.dma_start(out=hprev[:, :], in_=hbuf[rowe:rowe + 1, :, :])
                        self.inc("qG", i, 16)
                    self.block.gpsimd(_)
                    def _(s, Q=Q, t3end=t3end):
                        s.wait_ge(self.sem["tT"], t3end)
                        i = s.copy(ys[0:Q, :], QY[0:Q, :])
                        self.inc("tS", i)
                    self.block.scalar(_)
                    s4pos = self.cnt["tS"]
                    def _(t, Q=Q, s4pos=s4pos):
                        t.wait_ge(self.sem["tS"], s4pos)
                        for dsub in range(8):
                            i = t.transpose(QA[:, dsub * 128:dsub * 128 + Q],
                                            ys[0:Q, dsub * 128:dsub * 128 + 128],
                                            self.sb["idf32"][0:Q, 0:Q])
                        self.inc("tT", i)
                    self.block.tensor(_)
                    t4pos = self.cnt["tT"]
                    qs_prev_store = marks.get("qs_store", 0)
                    def _(s, Q=Q, t4pos=t4pos, qs_prev_store=qs_prev_store):
                        s.wait_ge(self.sem["tT"], t4pos)
                        if qs_prev_store:
                            s.wait_ge(self.sem["qS"], qs_prev_store)
                        for dsub in range(8):
                            i = s.copy(yts[:, dsub, 0:Q], QA[:, dsub * 128:dsub * 128 + Q])
                        self.inc("tS", i)
                    self.block.scalar(_)
                    s5end = self.cnt["tS"]
                    def _(q, dr=dr, hf=hf, c0=c0, Q=Q, s5end=s5end):
                        q.wait_ge(self.sem["tS"], s5end)
                        i = q.dma_start(out=self.dram["yT_sc"][dr, :, hf * 8:hf * 8 + 8, c0:c0 + Q],
                                        in_=yts[:, :, 0:Q])
                        self.inc("qS", i, 16)
                    self.block.sync(_)
                    marks["qs_store"] = self.cnt["qS"]
                    marks["prev"] = dict(sEvac=sEvac, vB=vB, VP=VP, qg_m=qg_m, s5end=s5end)
                    marks[ci] = (self.cnt["tT"], self.cnt["tS"], self.cnt["tV"])
        self.phase_barrier()

def pd(self: Prog):
    ntile = self.cfg["n_tiles"]
    with contextlib.ExitStack() as pex:
        sb = lambda n, s, d=F32: self.sbuf(n, s, d, ex=pex)
        ytt = sb("pd_ytt", [128, 16, TT])
        xct = sb("pd_xct", [128, 16, TT], BF16)
        szt = sb("pd_szt", [128, 16, TT])
        dpx = sb("pd_dpx", [128, 16, TT])
        yft = sb("pd_yft", [128, 16, TT])
        ygt = sb("pd_ygt", [128, 16, TT], BF16)
        for dr in self.cfg["dirs"]:
            for it in range(ntile):
                t0 = it * TT
                def _(q, dr=dr, t0=t0):
                    self.wait(q, "tS")
                    self.wait(q, "tV")
                    i = q.dma_start(out=ytt[:, :, :], in_=self.dram["yT_sc"][dr, :, :, t0:t0 + TT])
                    self.inc("qS", i, 16)
                    i = q.dma_start(out=xct[:, :, :], in_=self.dram["xcb_sc"][dr, :, :, t0:t0 + TT])
                    self.inc("qS", i, 16)
                    i = q.dma_start(out=szt[:, :, :], in_=self.dram["szf_sc"][dr, :, :, t0:t0 + TT])
                    self.inc("qS", i, 16)
                self.block.sync(_)
                def _(s, dr=dr):
                    self.wait(s, "qS")
                    self.wait(s, "tV")
                    for b in range(16):
                        i = s.activation(dpx[:, b, :], xct[:, b, :], AF.Copy,
                                         scale=self.bankcol("dp", dr * 16 + b))
                    self.inc("tS", i)
                self.block.scalar(_)
                def _(v, dr=dr):
                    self.wait(v, "tS")
                    self.wait(v, "qS")
                    i = v.tensor_tensor(yft[:, :, :], ytt[:, :, :], dpx[:, :, :], OP.add)
                    self.edge(v, "tV", i)
                    i = v.tensor_tensor(ygt[:, :, :], yft[:, :, :], szt[:, :, :], OP.mult)
                    self.inc("tV", i)
                self.block.vector(_)
                def _(q, dr=dr, t0=t0):
                    self.wait(q, "tV")
                    i = q.dma_start(out=self.dram["yg_sc"][dr, :, :, t0:t0 + TT], in_=ygt[:, :, :])
                    self.inc("qS", i, 16)
                self.block.sync(_)
        self.phase_barrier()

    with contextlib.ExitStack() as pex:
        sb = lambda n, s, d=F32: self.sbuf(n, s, d, ex=pex)
        w_op = sb("pd_wop", [128, 2, 16, D], BF16)
        ygf = sb("pd_ygf", [128, 16, TT], BF16)
        ygb = sb("pd_ygb", [128, 16, TT], BF16)
        xft = sb("pd_xft", [128, 8, TT])
        xtt = sb("pd_xtt", [128, 8, TT])
        t1 = sb("pd_t1", [128, TT])
        ost = sb("pd_ost", [128, 8, TT])
        PP = [self.psum("pd_P0", [128, 512], ex=pex), self.psum("pd_P1", [128, 512], ex=pex)]
        def _(g):
            self.wait(g, "tT")
            for dr in self.cfg["dirs"]:
                i = g.dma_start(out=w_op[:, dr, :, :], in_=self.dram["w_outproj"][dr].rearrange("(k p) m -> p k m", p=128))
                self.inc("qG", i, 16)
        self.block.gpsimd(_)
        xTv = self.dram["xT"].rearrange("(f p) t -> p f t", p=128)
        for it in range(ntile):
            t0 = it * TT
            def _(q, t0=t0):
                self.wait(q, "tT")
                self.wait(q, "tV")
                self.wait(q, "tS")
                i = q.dma_start(out=ygf[:, :, :], in_=self.dram["yg_sc"][0, :, :, t0:t0 + TT])
                self.inc("qS", i, 16)
                if len(self.cfg["dirs"]) > 1:
                    i = q.dma_start(out=ygb[:, :, :], in_=self.dram["yg_sc"][1, :, :, t0:t0 + TT])
                    self.inc("qS", i, 16)
                i = q.dma_start(out=xft[:, :, :], in_=self.dram["xfA"][:, :, 1 + t0:1 + t0 + TT])
                self.inc("qS", i, 16)
                i = q.dma_start(out=xtt[:, :, :], in_=xTv[:, :, t0:t0 + TT])
                self.inc("qS", i, 16)
            self.block.sync(_)
            tB = self.cnt["tT"]
            sB = self.cnt["tV"]
            def _(t, tB=tB, sB=sB):
                self.wait(t, "qS")
                self.wait(t, "qG")
                self.wait(t, "tV")
                for mt in range(8):
                    if mt >= 2:
                        t.wait_ge(self.sem["tV"], sB + 2 * (mt - 1))
                    pp = PP[mt % 2]
                    first = True
                    for dr in self.cfg["dirs"]:
                        yg = ygf if dr == 0 else ygb
                        for kb in range(16):
                            i = t.matmul(pp, w_op[:, dr, kb, mt * 128:mt * 128 + 128], yg[:, kb, :],
                                         start=first, stop=(dr == self.cfg["dirs"][-1] and kb == 15))
                            first = False
                    self.inc("tT", i)
            self.block.tensor(_)
            def _(v, tB=tB):
                for mt in range(8):
                    v.wait_ge(self.sem["tT"], tB + mt + 1)
                    pp = PP[mt % 2]
                    i = v.tensor_tensor(t1[:, :], pp, xft[:, mt, :], OP.add)
                    self.edge(v, "tV", i)
                    i = v.tensor_tensor(ost[:, mt, :], t1[:, :], xtt[:, mt, :], OP.add)
                    self.edge(v, "tV", i)
            self.block.vector(_)
            def _(q, t0=t0):
                self.wait(q, "tV")
                i = q.dma_start(out=self.dram["outT"].rearrange("(f p) t -> p f t", p=128)[:, :, t0:t0 + TT],
                                in_=ost[:, :, :])
                self.inc("qS", i, 16)
            self.block.sync(_)
        self.phase_barrier()

Prog.pb_main = pb_main
Prog.pc = pc
Prog.pd = pd



def kernel(**inputs):
    consts = _host_consts()
    w = _host_weights(inputs)
    x = np.asarray(inputs["L0_attn_x"])
    prog = Prog(CFG)
    nc = prog.build()
    shared = {}
    for k, v in list(w.items()) + list(consts.items()):
        shared[k] = np.ascontiguousarray(v.astype(np.float32))
    in_maps = []
    for c in range(B):
        m = dict(shared)
        m["xT"] = np.ascontiguousarray(x[c].T)
        in_maps.append(m)
    res = run_bass_kernel_spmd(nc, in_maps, core_ids=list(range(B)))
    out = np.empty((B, T, D), dtype=np.float32)
    for c in range(B):
        out[c] = res.results[c]["outT"].T
    return out


# version 08
# speedup vs baseline: 1.6965x; 1.1237x over previous
import contextlib
import numpy as np

import concourse.bass as bass
import concourse.mybir as mybir
from concourse.bass_utils import run_bass_kernel_spmd

F32 = mybir.dt.float32
BF16 = mybir.dt.bfloat16
AF = mybir.ActivationFunctionType
OP = mybir.AluOpType

B = 8
T = 4096
D = 1024
DI = 2048
DS = 16
DTR = 64
DCONV = 4
NL = 6
H = 16
DH = 64
W = 256
DFF = 2730
QK_SCALE = 8.0
EXP_BIAS = -24.0
NT = 8
TT = 512
QCH = 127
NCH_FULL = 32
QLAST = T - NCH_FULL * QCH

DEBUG_TAPS = ()
CFG = dict(n_tiles=NT, n_layers=NL, dirs=(0, 1), do_pa=True, do_pb=True,
           do_pc=True, do_pd=True, ffn_identity=False, n_chunks=NCH_FULL + 1,
           dbg_no_mask=False, dbg_no_vtr=False, dbg_no_attn=False)



def _host_consts():
    c = {}
    c["idf32"] = np.eye(128, dtype=np.float32)
    c["onescol"] = np.ones((128, 1), dtype=np.float32)
    c["onesrow"] = np.ones((1, 128), dtype=np.float32)
    bank = np.zeros(16, dtype=np.float32)
    bank[:10] = [1.0, 1.0 / D, 1e-5, -0.5, QK_SCALE, EXP_BIAS, 1e-12, 0.0, -1.0, 1e-24]
    c["bank"] = np.tile(bank[None, :], (128, 1))
    P = np.zeros((128, 128), dtype=np.float32)
    for base in (0, 64):
        for i in range(32):
            P[base + i + 32, base + i] = -1.0
            P[base + i, base + i + 32] = 1.0
    c["perm"] = P
    def mk_scan_mats(Q, fwd, ref):
        L2 = np.zeros((Q, Q + 1), dtype=np.float32)
        Mh = np.zeros((Q + 1, Q), dtype=np.float32)
        for k in range(Q):
            refin = (k <= ref) if fwd else (k >= ref)
            L2[k, Q] = -float(refin)
            for t in range(Q):
                inc = (k <= t) if fwd else (k >= t)
                L2[k, t] = float(inc) - float(refin)
                Mh[k, t] = float(inc)
        Mh[Q, :] = 1.0
        return L2, Mh
    c["L2f"], c["Mf"] = mk_scan_mats(QCH, True, 63)
    c["L2b"], c["Mb"] = mk_scan_mats(QCH, False, 63)
    c["L2fs"], c["Mfs"] = mk_scan_mats(QLAST, True, QLAST // 2 - 1)
    c["L2bs"], c["Mbs"] = mk_scan_mats(QLAST, False, QLAST // 2)
    half = np.arange(0, DH, 2, dtype=np.float64)
    inv_freq = 1.0 / 10000.0 ** (half / DH)
    t = np.arange(T, dtype=np.float64)
    freqs = t[:, None] * inv_freq[None, :]
    freqs = np.concatenate([freqs, freqs], axis=1)
    base_scale = (half + 0.4 * DH) / (1.4 * DH)
    power = (t - T // 2) / (W // 2)
    scale = base_scale[None, :] ** power[:, None]
    scale = np.concatenate([scale, scale], axis=1)
    cs_q = (np.cos(freqs) * scale).T.astype(np.float32)
    sn_q = (np.sin(freqs) * scale).T.astype(np.float32)
    cs_k = (np.cos(freqs) / scale).T.astype(np.float32)
    sn_k = (np.sin(freqs) / scale).T.astype(np.float32)
    c["tabs"] = np.ascontiguousarray(np.stack(
        [np.concatenate([x, x], axis=0) for x in (cs_q, sn_q, cs_k, sn_k)]))
    hind = np.zeros((8, 128, 16), dtype=np.float32)
    for f in range(8):
        hind[f, 0:64, 2 * f] = 1.0
        hind[f, 64:128, 2 * f + 1] = 1.0
    c["hind"] = hind
    c["hindT"] = np.ascontiguousarray(np.transpose(hind, (0, 2, 1)))
    return c


def _bank_idx():
    idx = {}
    o = 0
    for name, n in (("cw", 2 * 4 * 16), ("cb", 2 * 16), ("dtb", 2 * 16),
                    ("dp", 2 * 16), ("lng", NL * 8), ("lnb", NL * 8),
                    ("b2", NL * 8), ("b1a", NL * 22), ("b1g", NL * 22),
                    ("qsc", NL * 4)):
        idx[name] = o
        o += n
    idx["n"] = o
    return idx

BANK = _bank_idx()


def _host_weights(inp):
    w = {}
    w["w_inproj"] = np.ascontiguousarray(np.transpose(inp["L0_attn_in_proj_w"], (0, 2, 1)))
    w["w_xproj"] = np.ascontiguousarray(np.transpose(inp["L0_attn_x_proj_w"], (0, 2, 1)))
    w["w_dtproj"] = np.ascontiguousarray(np.transpose(inp["L0_attn_dt_proj_w"], (0, 2, 1)))
    w["w_outproj"] = np.ascontiguousarray(np.transpose(inp["L0_attn_out_proj_w"], (0, 2, 1)))
    w["w_qkv"] = np.ascontiguousarray(inp["L0_ffn_Wqkv"])
    w["w_o"] = np.ascontiguousarray(inp["L0_ffn_Wo"])
    w["w_1"] = np.ascontiguousarray(inp["L0_ffn_W1"])
    w["w_2"] = np.ascontiguousarray(inp["L0_ffn_W2"])
    w["alogT"] = np.ascontiguousarray(np.transpose(inp["L0_attn_A_log"], (0, 2, 1)))
    cols = []
    cw = np.transpose(inp["L0_attn_conv_w"], (0, 2, 1))
    for d in range(2):
        for k in range(4):
            cols.append(cw[d, k].reshape(16, 128).T)
    for name in ("L0_attn_conv_b", "L0_attn_dt_proj_b", "L0_attn_D_param"):
        for d in range(2):
            cols.append(inp[name][d].reshape(16, 128).T)
    for name in ("L0_ffn_ln_g", "L0_ffn_ln_b", "L0_ffn_b2"):
        for i in range(NL):
            cols.append(inp[name][i].reshape(8, 128).T)
    b1 = inp["L0_ffn_b1"]
    for i in range(NL):
        a = np.zeros((128, 22), dtype=np.float32)
        g = np.zeros((128, 22), dtype=np.float32)
        for j in range(22):
            n = min(128, DFF - j * 128)
            a[:n, j] = b1[i, j * 128: j * 128 + n]
            g[:n, j] = b1[i, DFF + j * 128: DFF + j * 128 + n]
        cols.append(a)
        cols.append(g)
    qs = inp["L0_ffn_q_scale"]
    ks = inp["L0_ffn_k_scale"]
    rotidx = (np.arange(128) % 64) ^ 32
    for i in range(NL):
        qsd = np.tile(qs[i], 2)
        ksd = np.tile(ks[i], 2)
        blk = np.stack([qsd, qsd[rotidx % 64 + (np.arange(128) // 64) * 64],
                        ksd, ksd[rotidx % 64 + (np.arange(128) // 64) * 64]], axis=1)
        cols.append(blk.astype(np.float32))
    w["smalls"] = np.ascontiguousarray(np.concatenate(cols, axis=1)).astype(np.float32)
    return w


class Prog:
    def __init__(self, cfg):
        self.cfg = cfg
        self.nc = bass.Bass(trn_type="TRN2", target_bir_lowering=False)
        self.ex = contextlib.ExitStack()
        self.sem = {}
        self.cnt = {}
        self.dram = {}
        self.sb = {}
        self.ps = {}

    def semaphore(self, name):
        h = self.ex.enter_context(self.nc.semaphore(name))
        self.sem[name] = h
        self.cnt[name] = 0
        return h

    def inc(self, name, instr, n=1):
        instr.then_inc(self.sem[name], n)
        self.cnt[name] += n
        return self.cnt[name]

    def wait(self, eng, name, val=None):
        v = self.cnt[name] if val is None else val
        if v > 0:
            eng.wait_ge(self.sem[name], v)

    def dt_in(self, name, shape, dt=F32):
        self.dram[name] = self.nc.dram_tensor(name, list(shape), dt, kind="ExternalInput").ap()
        return self.dram[name]

    def dt_out(self, name, shape, dt=F32):
        self.dram[name] = self.nc.dram_tensor(name, list(shape), dt, kind="ExternalOutput").ap()
        return self.dram[name]

    def dt_tmp(self, name, shape, dt=F32):
        self.dram[name] = self.nc.dram_tensor(name, list(shape), dt, kind="Internal").ap()
        return self.dram[name]

    def sbuf(self, name, shape, dt=F32, ex=None):
        t = (ex or self.ex).enter_context(self.nc.sbuf_tensor("sb_" + name, list(shape), dt))
        self.sb[name] = t.ap()
        return self.sb[name]

    def edge(self, eng, name, instr):
        n = self.inc(name, instr)
        eng.wait_ge(self.sem[name], n)

    def phase_barrier(self):
        sems = ("tT", "tS", "tV", "tG", "qS", "qG", "w1a", "w1b", "pcA", "pcB", "ysA", "ysB")
        for emit in (self.block.tensor, self.block.scalar, self.block.vector,
                     self.block.gpsimd, self.block.sync):
            def _(e):
                for s in sems:
                    self.wait(e, s)
            emit(_)

    def psum(self, name, shape, dt=F32, ex=None):
        t = (ex or self.ex).enter_context(self.nc.psum_tensor(name, list(shape), dt))
        self.ps[name] = t.ap()
        return self.ps[name]

    def bankcol(self, key, j):
        i = BANK[key] + j
        return self.sb["smalls"][:, i:i + 1]

    def const(self, which, p1=False):
        j = {"one": 0, "inv1024": 1, "lneps": 2, "neghalf": 3, "qks": 4,
             "expb": 5, "l2eps": 6, "zero": 7, "negone": 8, "l2eps2": 9}[which]
        ap = self.sb["bank"][:, j:j + 1]
        return ap[0:1, :] if p1 else ap

    def declare(self):
        self.dt_in("xT", [D, T])
        self.dt_in("w_inproj", [2, D, 2 * DI])
        self.dt_in("w_xproj", [2, DI, 96])
        self.dt_in("w_dtproj", [2, DTR, DI])
        self.dt_in("w_outproj", [2, DI, D])
        self.dt_in("w_qkv", [NL, D, 3 * D])
        self.dt_in("w_o", [NL, D, D])
        self.dt_in("w_1", [NL, D, 2 * DFF])
        self.dt_in("w_2", [NL, DFF, D])
        self.dt_in("alogT", [2, DS, DI])
        self.dt_in("smalls", [128, BANK["n"]])
        self.dt_in("hindT", [8, 16, 128])
        self.dt_in("tabs", [4, 128, T])
        self.dt_in("idf32", [128, 128])
        self.dt_in("onescol", [128, 1])
        self.dt_in("onesrow", [1, 128])
        self.dt_in("bank", [128, 16])
        self.dt_in("perm", [128, 128])
        self.dt_in("hind", [8, 128, 16])
        for nm, q in (("f", QCH), ("b", QCH), ("fs", QLAST), ("bs", QLAST)):
            self.dt_in("L2" + nm, [q, q + 1])
            self.dt_in("M" + nm, [q + 1, q])

        self.dt_out("outT", [D, T])

        self.dt_tmp("xnb_sc", [128, 8, T], BF16)
        self.dt_tmp("xfA", [128, 8, T + 2])
        self.dt_tmp("xfB", [128, 8, T + 2])
        self.dt_tmp("dtT_sc", [2, T, DI])
        self.dt_tmp("uT_sc", [2, T, DI], BF16)
        self.dt_tmp("xcb_sc", [2, 128, 16, T], BF16)
        self.dt_tmp("szf_sc", [2, 128, 16, T])
        self.dt_tmp("bct_sc", [2, T, 32])
        self.dt_tmp("yTok_sc", [2, T, DI])
        self.dt_tmp("yg_sc", [2, 128, 16, T], BF16)

        self.sbuf("smalls", [128, BANK["n"]])
        self.sbuf("bank", [128, 16])
        self.sbuf("idf32", [128, 128])
        self.sbuf("idbf", [128, 128], BF16)
        self.sbuf("onescol", [128, 1])
        self.sbuf("onesrow", [1, 128])
        self.sbuf("onesrow_b", [1, 128], BF16)
        self.sbuf("perm", [128, 128], BF16)
        self.sbuf("hind", [128, 8, 16], BF16)
        self.sbuf("hindT", [16, 8, 128], BF16)

        for s in ("qS", "qG", "tT", "tS", "tV", "tG", "w1a", "w1b", "pcA", "pcB", "ysA", "ysB"):
            self.semaphore(s)

    def p0(self):
        d = self.dram
        sb = self.sb
        def _(g):
            for src, dst in (("smalls", "smalls"), ("bank", "bank"), ("idf32", "idf32"),
                             ("idf32", "idbf"), ("onescol", "onescol"), ("onesrow", "onesrow"),
                             ("onesrow", "onesrow_b"), ("perm", "perm")):
                i = g.dma_start(out=sb[dst], in_=d[src])
                self.inc("qG", i, 16)
            i = g.dma_start(out=sb["hind"], in_=d["hind"].rearrange("f p m -> p f m"))
            self.inc("qG", i, 16)
            i = g.dma_start(out=sb["hindT"], in_=d["hindT"].rearrange("f m p -> m f p"))
            self.inc("qG", i, 16)
        self.block.gpsimd(_)
        self.p0_wait = self.cnt["qG"]

    def build(self):
        self.declare()
        with self.ex:
            self.block = self.ex.enter_context(self.nc.Block())
            self.p0()
            if self.cfg["do_pa"]:
                self.pa1()
                self.pa23()
            if self.cfg["do_pb"]:
                self.pb_main()
            if self.cfg["do_pc"]:
                self.pc()
            if self.cfg["do_pd"]:
                self.pd()
            self.emit_taps()
            self.finish()
        return self.nc

    def emit_taps(self):
        if not DEBUG_TAPS:
            return
        for name in DEBUG_TAPS:
            src = self.dram[name]
            self.dt_out("tap_" + name, list(src.shape), src.dtype)
        def _(q):
            for s in ("tT", "tS", "tV", "tG", "qG", "qS"):
                self.wait(q, s)
            for name in DEBUG_TAPS:
                i = q.dma_start(out=self.dram["tap_" + name], in_=self.dram[name])
                self.inc("qS", i, 16)
        self.block.sync(_)

    def finish(self):
        def _(q):
            for s in ("tT", "tS", "tV", "tG", "qG", "w1a", "w1b", "pcA", "pcB", "ysA", "ysB"):
                self.wait(q, s)
            self.wait(q, "qS")
        self.block.sync(_)


def pa1(self: Prog):
    ntile = self.cfg["n_tiles"]
    with contextlib.ExitStack() as pex:
        sb = lambda n, s, d=F32: self.sbuf(n, s, d, ex=pex)
        xs = sb("pa_xs", [128, 8, TT])
        xq = sb("pa_xq", [128, 8, TT])
        rows = sb("pa_rows", [1, 6 * TT])
        xt1 = sb("pa_xt1", [128, 8, TT])
        xn = sb("pa_xn", [128, 8, TT])
        xnb = sb("pa_xnb", [128, 8, TT], BF16)
        Q0 = self.psum("pa1_Q0", [128, 1024], ex=pex)
        Q1 = self.psum("pa1_Q1", [128, 1024], ex=pex)
        ps1 = Q0[0:1, 0:TT]
        ps2 = Q0[0:1, TT:2 * TT]
        prb = Q1[:, 0:TT]
        pmb = Q1[:, TT:2 * TT]
        xTv = self.dram["xT"].rearrange("(f p) t -> p f t", p=128)
        mu, mu2, s2s, var, rstd, mrs = (rows[:, i * TT:(i + 1) * TT] for i in range(6))

        for it in range(ntile):
            t0 = it * TT
            def _(q, t0=t0):
                self.wait(q, "qG", self.p0_wait)
                self.wait(q, "tT")
                self.wait(q, "tS")
                self.wait(q, "tV")
                i = q.dma_start(out=xs[:, :, :], in_=xTv[:, :, t0:t0 + TT])
                self.inc("qS", i, 16)
            self.block.sync(_)
            def _(t):
                self.wait(t, "qS")
                self.wait(t, "tS")
                for f in range(8):
                    i = t.matmul(ps1, self.sb["onescol"], xs[:, f, :], start=(f == 0), stop=(f == 7))
                self.inc("tT", i)
            self.block.tensor(_)
            def _(s):
                self.wait(s, "qS")
                i = s.activation(xq[:, :, :], xs[:, :, :], AF.Square)
                self.inc("tS", i)
            self.block.scalar(_)
            def _(t):
                self.wait(t, "tS")
                for f in range(8):
                    i = t.matmul(ps2, self.sb["onescol"], xq[:, f, :], start=(f == 0), stop=(f == 7))
                self.inc("tT", i)
            self.block.tensor(_)
            def _(s):
                self.wait(s, "tT")
                self.edge(s, "tS", s.activation(mu, ps1, AF.Copy, scale=self.const("inv1024", True)))
                s.activation(s2s, ps2, AF.Copy, scale=self.const("inv1024", True))
                i = s.activation(mu2, mu, AF.Square)
                self.inc("tS", i)
            self.block.scalar(_)
            def _(v):
                self.wait(v, "tS")
                i = v.tensor_tensor(var, s2s, mu2, OP.subtract)
                self.inc("tV", i)
            self.block.vector(_)
            def _(s):
                self.wait(s, "tV")
                self.edge(s, "tS", s.activation(var, var, AF.Ln, bias=self.const("lneps", True)))
                i = s.activation(rstd, var, AF.Exp, scale=self.const("neghalf", True))
                self.inc("tS", i)
            self.block.scalar(_)
            def _(v):
                self.wait(v, "tS")
                i = v.tensor_tensor(mrs, mu, rstd, OP.mult)
                self.inc("tV", i)
            self.block.vector(_)
            def _(t):
                self.wait(t, "tV")
                t.matmul(prb, self.sb["onesrow"], rstd, start=True, stop=True)
                i = t.matmul(pmb, self.sb["onesrow"], mrs, start=True, stop=True)
                self.inc("tT", i)
            self.block.tensor(_)
            def _(v):
                self.wait(v, "tT")
                for f in range(8):
                    i = v.tensor_tensor(xt1[:, f, :], xs[:, f, :], prb, OP.mult)
                self.edge(v, "tV", i)
                for f in range(8):
                    i = v.tensor_tensor(xn[:, f, :], xt1[:, f, :], pmb, OP.subtract)
                self.inc("tV", i)
            self.block.vector(_)
            def _(s):
                self.wait(s, "tV")
                i = s.copy(xnb[:, :, :], xn[:, :, :])
                self.inc("tS", i)
            self.block.scalar(_)
            def _(q, t0=t0):
                self.wait(q, "tV")
                i = q.dma_start(out=self.dram["xfA"][:, :, 1 + t0:1 + t0 + TT], in_=xn[:, :, :])
                self.inc("qS", i, 16)
                self.wait(q, "tS")
                i = q.dma_start(out=self.dram["xnb_sc"][:, :, t0:t0 + TT], in_=xnb[:, :, :])
                self.inc("qS", i, 16)
            self.block.sync(_)

        def _(v):
            self.wait(v, "tT")
            self.wait(v, "tS")
            i = v.memset(xs[:, :, 0:1], 0.0)
            self.inc("tV", i)
        self.block.vector(_)
        def _(q):
            self.wait(q, "tV")
            for nm in ("xfA", "xfB"):
                i = q.dma_start(out=self.dram[nm][:, :, 0:1], in_=xs[:, :, 0:1],
                                allow_slow_non_contiguous=True)
                self.inc("qS", i, 16)
        self.block.sync(_)
        self.phase_barrier()


TTA = 256

def pa23(self: Prog):
    ntile = T // TTA if self.cfg["n_tiles"] == NT else self.cfg["n_tiles"]
    with contextlib.ExitStack() as pex:
        sb = lambda n, s, d=F32: self.sbuf(n, s, d, ex=pex)
        w_ip = sb("pa_wip", [128, 8, 2 * DI], BF16)
        w_xp = sb("pa_wxp", [128, 16, 96], BF16)
        w_dt = sb("pa_wdt", [DTR, DI], BF16)
        dg = sb("pa_dg", [128, 4 * 16 * 128], BF16)
        xnbt = sb("pa_xnbt", [128, 8, TTA], BF16)
        xi = sb("pa_xi", [128, 16, TTA + 4], BF16)
        xihalo = sb("pa_xihalo", [128, 16, 4], BF16)
        xcb = sb("pa_xcb", [128, 16, TTA], BF16)
        szf = sb("pa_szf", [128, 16, TTA])
        dtraw = sb("pa_dtraw", [DTR, TTA], BF16)
        bc = sb("pa_bc", [32, TTA])
        bct = sb("pa_bct", [128, 2, 32])
        et = sb("pa_et", [128, TTA])
        dts = sb("pa_dts", [128, 16, TTA])
        dtb16 = sb("pa_dtb16", [128, 16, TTA], BF16)
        ub = sb("pa_ub", [128, 16, TTA], BF16)
        trf = sb("pa_trf", [128, DI])
        trb = sb("pa_trb", [128, DI], BF16)
        dgv = dg.rearrange("p (k b m) -> p k b m", k=4, b=16)
        Q0 = self.psum("pa_Q0", [128, 1024], ex=pex)
        Q1 = self.psum("pa_Q1", [128, 1024], ex=pex)
        P0 = self.psum("pa_P0", [128, 512], ex=pex)
        P1 = self.psum("pa_P1", [128, 512], ex=pex)
        PB16 = self.psum("pa_PB16", [128, 2048], BF16, ex=pex)

        for dr in self.cfg["dirs"]:
            def _(g, dr=dr):
                self.wait(g, "qG", self.p0_wait)
                self.wait(g, "tT")
                i = g.dma_start(out=w_ip[:, :, :],
                                in_=self.dram["w_inproj"][dr].rearrange("(k p) m -> p k m", p=128))
                self.inc("qG", i, 16)
                i = g.dma_start(out=w_xp[:, :, :],
                                in_=self.dram["w_xproj"][dr].rearrange("(k p) m -> p k m", p=128))
                self.inc("qG", i, 16)
                i = g.dma_start(out=w_dt[:, :], in_=self.dram["w_dtproj"][dr])
                self.inc("qG", i, 16)
            self.block.gpsimd(_)
            def _(v, dr=dr):
                self.wait(v, "qG")
                self.wait(v, "tT")
                for k in range(4):
                    for b in range(16):
                        i = v.tensor_scalar(dgv[:, k, b, :], self.sb["idbf"],
                                            self.bankcol("cw", dr * 64 + k * 16 + b), None, OP.mult)
                self.inc("tV", i)
            self.block.vector(_)
            def _(s, dr=dr):
                self.wait(s, "tV")
                i = s.memzero(xihalo[:, :, :])
                self.inc("tS", i)
            self.block.scalar(_)

            tiles = list(range(ntile)) if dr == 0 else list(range(ntile - 1, -1, -1))
            for ti, it in enumerate(tiles):
                t0 = it * TTA
                xioff = 3 if dr == 0 else 0
                def _(q, t0=t0):
                    self.wait(q, "tT")
                    self.wait(q, "tS")
                    self.wait(q, "tV")
                    i = q.dma_start(out=xnbt[:, :, :],
                                    in_=self.dram["xnb_sc"][:, :, t0:t0 + TTA])
                    self.inc("qS", i, 16)
                self.block.sync(_)

                tB = self.cnt["tT"]; sB = self.cnt["tS"]
                def _(t, tB=tB, sB=sB):
                    self.wait(t, "qS")
                    self.wait(t, "qG")
                    self.wait(t, "tV")
                    self.wait(t, "tS")
                    for mt in range(32):
                        if mt >= 2:
                            t.wait_ge(self.sem["tS"], sB + mt - 1)
                        pp = Q0[:, (mt % 2) * 512:(mt % 2) * 512 + TTA]
                        for kb in range(8):
                            i = t.matmul(pp, w_ip[:, kb, mt * 128:mt * 128 + 128], xnbt[:, kb, :],
                                         start=(kb == 0), stop=(kb == 7))
                        self.inc("tT", i)
                self.block.tensor(_)
                def _(s, dr=dr, tB=tB, xioff=xioff):
                    self.wait(s, "qS")
                    for mt in range(32):
                        s.wait_ge(self.sem["tT"], tB + mt + 1)
                        pp = Q0[:, (mt % 2) * 512:(mt % 2) * 512 + TTA]
                        if mt < 16:
                            i = s.copy(xi[:, mt, xioff:xioff + TTA], pp)
                        else:
                            i = s.activation(szf[:, mt - 16, :], pp, AF.Silu)
                        self.inc("tS", i)
                self.block.scalar(_)

                def _(v, dr=dr):
                    self.wait(v, "tS")
                    if dr == 0:
                        self.edge(v, "tV", v.tensor_copy(xi[:, :, 0:3], xihalo[:, :, 0:3]))
                        i = v.tensor_copy(xihalo[:, :, 0:3], xi[:, :, TTA:TTA + 3])
                    else:
                        self.edge(v, "tV", v.tensor_copy(xi[:, :, TTA:TTA + 3], xihalo[:, :, 0:3]))
                        i = v.tensor_copy(xihalo[:, :, 0:3], xi[:, :, 0:3])
                    self.inc("tV", i)
                self.block.vector(_)

                tB = self.cnt["tT"]; sB = self.cnt["tS"]
                def _(t, dr=dr, tB=tB, sB=sB):
                    self.wait(t, "tV")
                    for b in range(16):
                        if b >= 2:
                            t.wait_ge(self.sem["tS"], sB + b - 1)
                        pc = P0 if b % 2 == 0 else P1
                        for k in range(4):
                            off = k if dr == 0 else 3 - k
                            i = t.matmul(pc[:, 0:TTA], dgv[:, k, b, :], xi[:, b, off:off + TTA],
                                         start=(k == 0), stop=(k == 3))
                        self.inc("tT", i)
                self.block.tensor(_)
                def _(s, dr=dr, tB=tB):
                    for b in range(16):
                        s.wait_ge(self.sem["tT"], tB + b + 1)
                        pc = P0 if b % 2 == 0 else P1
                        i = s.activation(xcb[:, b, :], pc[:, 0:TTA], AF.Silu,
                                         bias=self.bankcol("cb", dr * 16 + b))
                        self.inc("tS", i)
                self.block.scalar(_)

                def _(t, dr=dr):
                    self.wait(t, "tS")
                    pxp = Q1[0:96, 0:TTA]
                    for kb in range(16):
                        i = t.matmul(pxp, w_xp[:, kb, :], xcb[:, kb, :],
                                     start=(kb == 0), stop=(kb == 15))
                    self.inc("tT", i)
                self.block.tensor(_)
                def _(s, dr=dr):
                    self.wait(s, "tT")
                    s.copy(dtraw[:, :], Q1[0:64, 0:TTA])
                    i = s.copy(bc[:, :], Q1[64:96, 0:TTA])
                    self.inc("tS", i)
                self.block.scalar(_)
                def _(t, dr=dr):
                    self.wait(t, "tS")
                    for j in range(2):
                        i = t.transpose(P0[:, j * 32:j * 32 + 32],
                                        bc[0:32, j * 128:j * 128 + 128],
                                        self.sb["idf32"][0:32, 0:32])
                    self.inc("tT", i)
                self.block.tensor(_)
                def _(s, dr=dr):
                    self.wait(s, "tT")
                    i = s.copy(bct[:, :, :], P0[:, 0:64].rearrange("p (j c) -> p j c", j=2))
                    self.inc("tS", i)
                self.block.scalar(_)
                def _(q, dr=dr, t0=t0):
                    self.wait(q, "tS")
                    i = q.dma_start(out=self.dram["bct_sc"][dr, t0:t0 + TTA, :].rearrange("(j p) c -> p j c", p=128),
                                    in_=bct[:, :, :])
                    self.inc("qS", i, 16)
                self.block.sync(_)

                tB = self.cnt["tT"]; sB = self.cnt["tS"]
                def _(t, dr=dr, tB=tB, sB=sB):
                    self.wait(t, "tS")
                    for mt in range(16):
                        if mt >= 2:
                            t.wait_ge(self.sem["tS"], sB + 3 * (mt - 1))
                        pd_ = Q1[:, (mt % 2) * 512:(mt % 2) * 512 + TTA]
                        i = t.matmul(pd_, w_dt[:, mt * 128:mt * 128 + 128], dtraw[:, :],
                                     start=True, stop=True)
                        self.inc("tT", i)
                self.block.tensor(_)
                def _(s, dr=dr, tB=tB):
                    for mt in range(16):
                        s.wait_ge(self.sem["tT"], tB + mt + 1)
                        pd_ = Q1[:, (mt % 2) * 512:(mt % 2) * 512 + TTA]
                        self.edge(s, "tS", s.activation(et[:, :], pd_, AF.Exp, bias=self.bankcol("dtb", dr * 16 + mt)))
                        self.edge(s, "tS", s.activation(dts[:, mt, :], et[:, :], AF.Ln, bias=self.const("one")))
                        i = s.copy(dtb16[:, mt, :], dts[:, mt, :])
                        self.inc("tS", i)
                self.block.scalar(_)
                def _(v, dr=dr):
                    self.wait(v, "tS")
                    i = v.tensor_tensor(ub[:, :, :], dtb16[:, :, :], xcb[:, :, :], OP.mult)
                    self.inc("tV", i)
                self.block.vector(_)
                def _(q, dr=dr, t0=t0):
                    self.wait(q, "tS")
                    i = q.dma_start(out=self.dram["xcb_sc"][dr, :, :, t0:t0 + TTA], in_=xcb[:, :, :])
                    self.inc("qS", i, 16)
                    i = q.dma_start(out=self.dram["szf_sc"][dr, :, :, t0:t0 + TTA], in_=szf[:, :, :])
                    self.inc("qS", i, 16)
                self.block.sync(_)

                for tsub in range(TTA // 128):
                    def _(t, dr=dr, tsub=tsub):
                        self.wait(t, "tS")
                        self.wait(t, "tV")
                        for dh in range(2):
                            pq = Q0 if dh == 0 else Q1
                            for dsub in range(8):
                                i = t.transpose(pq[:, dsub * 128:dsub * 128 + 128],
                                                dts[:, dh * 8 + dsub, tsub * 128:tsub * 128 + 128],
                                                self.sb["idf32"])
                            self.inc("tT", i)
                        for dd in range(16):
                            i = t.transpose(PB16[:, dd * 128:dd * 128 + 128],
                                            ub[:, dd, tsub * 128:tsub * 128 + 128],
                                            self.sb["idbf"])
                        self.inc("tT", i)
                    self.block.tensor(_)
                    def _(s, dr=dr, tsub=tsub):
                        self.wait(s, "tT")
                        self.wait(s, "qS")
                        s.copy(trf[:, 0:1024], Q0[:, :])
                        i = s.copy(trf[:, 1024:2048], Q1[:, :])
                        self.inc("tS", i)
                    self.block.scalar(_)
                    def _(v, dr=dr, tsub=tsub):
                        self.wait(v, "tT")
                        self.wait(v, "qS")
                        i = v.tensor_copy(trb[:, :], PB16[:, :])
                        self.inc("tV", i)
                    self.block.vector(_)
                    def _(q, dr=dr, t0=t0, tsub=tsub):
                        self.wait(q, "tS")
                        r0 = t0 + tsub * 128
                        i = q.dma_start(out=self.dram["dtT_sc"][dr, r0:r0 + 128, :], in_=trf[:, :])
                        self.inc("qS", i, 16)
                        self.wait(q, "tV")
                        i = q.dma_start(out=self.dram["uT_sc"][dr, r0:r0 + 128, :], in_=trb[:, :])
                        self.inc("qS", i, 16)
                    self.block.sync(_)
        self.phase_barrier()


Prog.pa1 = pa1
Prog.pa23 = pa23


def pb(self: Prog):
    ntile = self.cfg["n_tiles"]
    nlay = self.cfg["n_layers"]
    bank = self.sb["smalls"]
    for li in range(nlay):
        self.pb_attn(li, "xfA" if (li % 2 == 0) else "xfA", None)

def pb_main(self: Prog):
    ntile = self.cfg["n_tiles"]
    nlay = self.cfg["n_layers"]
    src_name = "xfA"
    dst_name = "xfB"
    for li in range(nlay):
        self.pb_attn(li, src_name, dst_name)
        src_name, dst_name = dst_name, src_name
        self.pb_ffn(li, src_name, dst_name)
        src_name, dst_name = dst_name, src_name


def pb_attn(self: Prog, li, srcn, dstn):
    ntile = self.cfg["n_tiles"]
    with contextlib.ExitStack() as pex:
        sb = lambda n, s, d=F32: self.sbuf(n + "_l%d" % li, s, d, ex=pex)
        wqkv = sb("ba_wqkv", [128, 8, 3 * D], BF16)
        wo = sb("ba_wo", [128, 8, D], BF16)
        tabs = sb("ba_tabs", [128, 2, TT], BF16)
        xf = sb("ba_xf", [128, 8, TT + 1])
        xsb = sb("ba_xsb", [128, 8, TT], BF16)
        qb = sb("ba_qb", [128, 8, TT], BF16)
        sq = sb("ba_sq", [128, 8, TT], BF16)
        t1b = sb("ba_t1", [128, 8, TT], BF16)
        t2b = sq
        vb = qb
        qh = sb("ba_qh", [128, H, TT], BF16)
        kring = sb("ba_kring", [128, H, 6, 128], BF16)
        vtr = sb("ba_vtr", [128, 6, H, 65], BF16)
        nr = sb("ba_nr", [16, TT])
        rnb16 = sb("ba_rnb", [16, TT], BF16)
        ew = sb("ba_ew", [128, 2, 4, W], BF16)
        avs = sb("ba_avs", [65, 2, W])
        rr = sb("ba_rr", [1, 2, W], BF16)
        rr32 = sb("ba_rr32", [1, 2, W])
        ao = sb("ba_ao", [128, 8, TT], BF16)
        xfp = xf
        QA = self.psum("ba_QA_l%d" % li, [128, 1024], ex=pex)
        QB = self.psum("ba_QB_l%d" % li, [128, 1024], ex=pex)
        PR = self.psum("ba_PR_l%d" % li, [128, 1024], ex=pex)
        PN = self.psum("ba_PN_l%d" % li, [128, 512], ex=pex)
        PV = self.psum("ba_PV_l%d" % li, [128, 1024], BF16, ex=pex)

        def _(g, li=li):
            self.wait(g, "tT")
            self.wait(g, "tV")
            i = g.dma_start(out=wqkv[:, :, :], in_=self.dram["w_qkv"][li].rearrange("(k p) m -> p k m", p=128))
            self.inc("qG", i, 16)
            i = g.dma_start(out=wo[:, :, :], in_=self.dram["w_o"][li].rearrange("(k p) m -> p k m", p=128))
            self.inc("qG", i, 16)
        self.block.gpsimd(_)
        def _(v, li=li):
            self.wait(v, "qG")
            v.memset(vtr[:, :, :, 64:65], 1.0)
            v.memset(qh[:, :, :], 0.0)
            i = v.memset(kring[:, :, :, :], 0.0)
            self.inc("tV", i)
        self.block.vector(_)

        for it in range(ntile):
            t0 = it * TT
            def _(q, t0=t0, srcn=srcn):
                self.wait(q, "tT")
                self.wait(q, "tS")
                self.wait(q, "tV")
                self.wait(q, "tG")
                self.wait(q, "qS")
                i = q.dma_start(out=xf[:, :, :], in_=self.dram[srcn][:, :, t0:t0 + TT + 1])
                self.inc("qS", i, 16)
            self.block.sync(_)
            def _(s):
                self.wait(s, "qS")
                for f in range(8):
                    i = s.copy(xsb[:, f, :], xf[:, f, 1:1 + TT] if f < 4 else xf[:, f, 0:TT])
                self.inc("tS", i)
            self.block.scalar(_)

            stg = self.cfg.get("dbg_stage", 9)
            if stg >= 2:
                self.pb_qk_section(li, it, t0, True, wqkv, xsb, qb, sq, t1b, t2b, None, qh, None,
                                   tabs, nr, rnb16, QA, QB, PR, PN)
            if stg >= 3:
                self.pb_qk_section(li, it, t0, False, wqkv, xsb, qb, sq, t1b, t2b, None, None, kring,
                                   tabs, nr, rnb16, QA, QB, PR, PN)
            if stg < 4:
                continue
            tB = self.cnt["tT"]
            sBv = self.cnt["tS"]
            def _(t, tB=tB, sBv=sBv):
                self.wait(t, "tS")
                self.wait(t, "tV")
                for mt in range(8):
                    if mt >= 2:
                        t.wait_ge(self.sem["tS"], sBv + mt - 1)
                    pp = QA[:, (mt % 2) * 512:(mt % 2) * 512 + 512]
                    if self.cfg.get("dbg_no_vmm"):
                        i = t.matmul(pp[:, 0:4], wqkv[:, 0, 2 * D:2 * D + 128],
                                     xsb[:, 0, 0:4], start=True, stop=True)
                    else:
                        for kb in range(8):
                            i = t.matmul(pp, wqkv[:, kb, 2 * D + mt * 128:2 * D + mt * 128 + 128],
                                         xsb[:, kb, :], start=(kb == 0), stop=(kb == 7))
                    self.inc("tT", i)
            self.block.tensor(_)
            def _(s, tB=tB):
                for mt in range(8):
                    s.wait_ge(self.sem["tT"], tB + mt + 1)
                    pp = QA[:, (mt % 2) * 512:(mt % 2) * 512 + 512]
                    if self.cfg.get("dbg_no_vevac"):
                        i = s.copy(vb[:, mt, 0:4], pp[:, 0:4])
                    else:
                        i = s.copy(vb[:, mt, :], pp)
                    self.inc("tS", i)
            self.block.scalar(_)
            tB = self.cnt["tT"]
            sB = self.cnt["tS"]
            def _(t, it=it, tB=tB, sB=sB):
                if self.cfg.get("dbg_no_vtr"):
                    for j in range(4):
                        self.inc("tT", t.nop())
                    return
                self.wait(t, "tS")
                for j in range(4):
                    if j >= 1:
                        t.wait_ge(self.sem["tS"], sB + j)
                    slot = (4 * it + j) % 6
                    for fb in range(8):
                        i = t.transpose(PV[:, fb * 128:fb * 128 + 128],
                                        vb[:, fb, j * 128:j * 128 + 128],
                                        self.sb["idbf"])
                    self.inc("tT", i)
            self.block.tensor(_)
            def _(s, it=it, tB=tB):
                for j in range(4):
                    slot = (4 * it + j) % 6
                    s.wait_ge(self.sem["tT"], tB + j + 1)
                    if self.cfg.get("dbg_no_vtrevac"):
                        i = s.copy(vtr[:, slot, 0, 0:4], PV[:, 0:4])
                    else:
                        i = s.copy(vtr[:, slot, :, 0:64], PV[:, 0:1024].rearrange("p (h c) -> p h c", h=H))
                    self.inc("tS", i)
            self.block.scalar(_)

            if stg < 5:
                continue
            seq = []
            for wloc in range(2):
                wg = 2 * it + wloc
                kb4s = [2 * wg - 2, 2 * wg - 1, 2 * wg, 2 * wg + 1]
                for h in range(H):
                    seq.append((wloc, wg, h, kb4s))
            n = len(seq)
            Tops, Sops, Gops, Vops = [], [], [], []
            for i in range(n):
                Tops.append(("sim", i))
                if i >= 1:
                    Tops.append(("av", i - 1))
                if i >= 2:
                    Tops.append(("rcpb", i - 2))
                Sops.append(("exp", i))
                if i >= 1:
                    Sops += [("evac", i - 1), ("ln", i - 1), ("rr", i - 1)]
                Gops.append(("mask", i))
                Vops.append(("div", i))
            Tops += [("av", n - 1), ("rcpb", n - 2), ("rcpb", n - 1)]
            Sops += [("evac", n - 1), ("ln", n - 1), ("rr", n - 1)]
            TC = {op: k + 1 for k, op in enumerate(Tops)}
            SC = {op: k + 1 for k, op in enumerate(Sops)}
            GC = {op: k + 1 for k, op in enumerate(Gops)}
            VC = {op: k + 1 for k, op in enumerate(Vops)}
            tB = self.cnt["tT"]; sB = self.cnt["tS"]; gB = self.cnt["tG"]; vB = self.cnt["tV"]
            simP = [QA, QB]
            avP = [PN[0:65, 0:W], PR[0:65, 0:W]]
            rcpbP = [PR[0:64, 512:512 + W], PR[0:64, 512:512 + W]]

            def _(t, seq=seq, Tops=Tops, TC=TC, SC=SC, GC=GC, VC=VC,
                  tB=tB, sB=sB, gB=gB, vB=vB):
                self.wait(t, "tV")
                self.wait(t, "tS")
                for op, i in Tops:
                    wloc, wg, h, kb4s = seq[i]
                    par = i % 2
                    q0 = wloc * W
                    valid = [k for k in range(4) if kb4s[k] >= 0]
                    if op == "sim":
                        if i >= 2:
                            t.wait_ge(self.sem["tS"], sB + SC[("exp", i - 2)])
                        for kb in valid:
                            slot = kb4s[kb] % 6
                            ii = t.matmul(simP[par][0:128, kb * 256:kb * 256 + 256],
                                          kring[:, h, slot, :], qh[:, h, q0:q0 + W],
                                          start=True, stop=True)
                        self.inc("tT", ii)
                    elif op == "av":
                        t.wait_ge(self.sem["tG"], gB + GC[("mask", i)])
                        if i >= 2:
                            t.wait_ge(self.sem["tS"], sB + SC[("evac", i - 2)])
                        first = True
                        for kb in valid:
                            slot = kb4s[kb] % 6
                            ii = t.matmul(avP[par], vtr[:, slot, h, :], ew[:, par, kb, :],
                                          start=first, stop=(kb == valid[-1]))
                            first = False
                        self.inc("tT", ii)
                    else:
                        t.wait_ge(self.sem["tS"], sB + SC[("rr", i)])
                        if i >= 1:
                            t.wait_ge(self.sem["tV"], vB + VC[("div", i - 1)])
                        ii = t.matmul(rcpbP[par], self.sb["onesrow_b"][0:1, 0:64],
                                      rr[:, par, :], start=True, stop=True)
                        self.inc("tT", ii)
            self.block.tensor(_)

            def _(s, seq=seq, Sops=Sops, TC=TC, VC=VC, tB=tB, vB=vB):
                for op, i in Sops:
                    par = i % 2
                    if op == "exp":
                        s.wait_ge(self.sem["tT"], tB + TC[("sim", i)])
                        if i >= 2:
                            s.wait_ge(self.sem["tT"], tB + TC[("av", i - 2)])
                        ii = s.activation(ew[:, par, :, :], simP[par][:, 0:1024], AF.Exp,
                                          scale=self.const("qks"), bias=self.const("expb"))
                        self.inc("tS", ii)
                    elif op == "evac":
                        s.wait_ge(self.sem["tT"], tB + TC[("av", i)])
                        if i >= 2:
                            s.wait_ge(self.sem["tV"], vB + VC[("div", i - 2)])
                        self.edge(s, "tS", s.copy(avs[:, par, :], avP[par]))
                    elif op == "ln":
                        self.edge(s, "tS", s.activation(rr32[:, par, :], avs[64:65, par, :], AF.Ln))
                    else:
                        with self.nc.allow_low_precision(reason="softmax denom bf16 operand"):
                            ii = s.activation(rr[:, par, :], rr32[:, par, :], AF.Exp,
                                              scale=self.const("negone")[0:1, :])
                        self.inc("tS", ii)
            self.block.scalar(_)

            def _(g, seq=seq, Gops=Gops, SC=SC, sB=sB):
                for op, i in Gops:
                    wloc, wg, h, kb4s = seq[i]
                    par = i % 2
                    g.wait_ge(self.sem["tS"], sB + SC[("exp", i)])
                    last = None
                    for kb in range(4):
                        if kb4s[kb] < 0:
                            continue
                        if kb < 2:
                            last = g.affine_select(ew[:, par, kb, :], ew[:, par, kb, :], [[-1, W]],
                                                   OP.is_ge, 0.0, base=kb * 128,
                                                   channel_multiplier=1)
                        else:
                            last = g.affine_select(ew[:, par, kb, :], ew[:, par, kb, :], [[1, W]],
                                                   OP.is_ge, 0.0, base=-(kb - 2) * 128,
                                                   channel_multiplier=-1)
                    self.inc("tG", last)
            self.block.gpsimd(_)

            def _(v, seq=seq, Vops=Vops, TC=TC, SC=SC, tB=tB, sB=sB):
                for op, i in Vops:
                    wloc, wg, h, kb4s = seq[i]
                    par = i % 2
                    q0 = wloc * W
                    fb = h // 2
                    pb_ = 64 * (h % 2)
                    v.wait_ge(self.sem["tT"], tB + TC[("rcpb", i)])
                    v.wait_ge(self.sem["tS"], sB + SC[("rr", i)])
                    ii = v.tensor_tensor(ao[pb_:pb_ + 64, fb, q0:q0 + W], avs[0:64, par, :],
                                         rcpbP[par][0:64, 0:W] if par == 0 else rcpbP[par][0:64, 0:W],
                                         OP.mult)
                    self.inc("tV", ii)
            self.block.vector(_)

            if stg < 6:
                continue
            tB = self.cnt["tT"]
            sB = self.cnt["tV"]
            def _(t, tB=tB, sB=sB):
                self.wait(t, "tV")
                for mt in range(8):
                    if mt >= 2:
                        t.wait_ge(self.sem["tV"], sB + mt - 1)
                    pp = QB[:, (mt % 2) * 512:(mt % 2) * 512 + 512]
                    for kb in range(8):
                        i = t.matmul(pp, wo[:, kb, mt * 128:mt * 128 + 128], ao[:, kb, :],
                                     start=(kb == 0), stop=(kb == 7))
                    self.inc("tT", i)
            self.block.tensor(_)
            def _(v, tB=tB):
                for mt in range(8):
                    v.wait_ge(self.sem["tT"], tB + mt + 1)
                    pp = QB[:, (mt % 2) * 512:(mt % 2) * 512 + 512]
                    i = v.tensor_tensor(xf[:, mt, 1:1 + TT], pp, xf[:, mt, 1:1 + TT], OP.add)
                    self.inc("tV", i)
            self.block.vector(_)
            def _(q, t0=t0, dstn=dstn):
                self.wait(q, "tV")
                i = q.dma_start(out=self.dram[dstn][:, :, 1 + t0:1 + t0 + TT], in_=xf[:, :, 1:1 + TT])
                self.inc("qS", i, 16)
            self.block.sync(_)
        self.phase_barrier()


def pb_qk_section(self: Prog, li, it, t0, is_q, wqkv, xsb, qb, sq, t1b, t2b, t3b, qh, kring,
                  tabs, nr, rnb16, QA, QB, PR, PN):
    off = 0 if is_q else D
    tj = 0 if is_q else 2
    def _(g, li=li, t0=t0, tj=tj):
        self.wait(g, "tV")
        for j in range(2):
            i = g.dma_start(out=tabs[:, j, :],
                            in_=self.dram["tabs"][tj + j, :, t0:t0 + TT])
            self.inc("qG", i, 16)
    self.block.gpsimd(_)
    def _(v, li=li, tj=tj):
        self.wait(v, "qG")
        for j in range(2):
            i = v.tensor_scalar(tabs[:, j, :], tabs[:, j, :],
                                self.bankcol("qsc", li * 4 + tj + j), None, OP.mult)
        self.inc("tV", i)
    self.block.vector(_)
    tB = self.cnt["tT"]
    sB = self.cnt["tS"]
    def _(t, tB=tB, sB=sB):
        self.wait(t, "tS")
        self.wait(t, "tV")
        for mt in range(8):
            if mt >= 2:
                t.wait_ge(self.sem["tS"], sB + 2 * (mt - 1))
            pp = QA[:, (mt % 2) * 512:(mt % 2) * 512 + 512]
            for kb in range(8):
                i = t.matmul(pp, wqkv[:, kb, off + mt * 128:off + mt * 128 + 128],
                             xsb[:, kb, :], start=(kb == 0), stop=(kb == 7))
            self.inc("tT", i)
    self.block.tensor(_)
    def _(s, tB=tB):
        for mt in range(8):
            s.wait_ge(self.sem["tT"], tB + mt + 1)
            pp = QA[:, (mt % 2) * 512:(mt % 2) * 512 + 512]
            i = s.activation(sq[:, mt, :], pp, AF.Square)
            self.inc("tS", i)
            i = s.copy(qb[:, mt, :], pp)
            self.inc("tS", i)
    self.block.scalar(_)
    def _(t):
        self.wait(t, "tS")
        for f in range(8):
            i = t.matmul(PN[0:16, :], self.sb["hind"][:, f, :], sq[:, f, :],
                         start=(f == 0), stop=(f == 7))
        self.inc("tT", i)
    self.block.tensor(_)
    def _(v):
        self.wait(v, "tT")
        i = v.tensor_scalar(nr[:, :], PN[0:16, :], self.const("l2eps2")[0:16, :], None, OP.max)
        self.inc("tV", i)
    self.block.vector(_)
    def _(s):
        self.wait(s, "tV")
        self.edge(s, "tS", s.activation(nr[:, :], nr[:, :], AF.Ln))
        i = s.activation(rnb16[:, :], nr[:, :], AF.Exp, scale=self.const("neghalf")[0:16, :])
        self.inc("tS", i)
    self.block.scalar(_)
    tB = self.cnt["tT"]
    vB = self.cnt["tV"]
    def _(t, tB=tB, vB=vB):
        self.wait(t, "tV")
        self.wait(t, "tS")
        for f in range(8):
            if f >= 2:
                t.wait_ge(self.sem["tV"], vB + 4 * f - 4)
            i = t.matmul(PR[:, (f % 2) * 512:(f % 2) * 512 + 512],
                         self.sb["hindT"][:, f, :], rnb16[:, :], start=True, stop=True)
            self.inc("tT", i)
            i = t.matmul(QB[:, (f % 2) * 512:(f % 2) * 512 + 512],
                         self.sb["perm"], qb[:, f, :], start=True, stop=True)
            self.inc("tT", i)
    self.block.tensor(_)
    def _(v, it=it, t0=t0, tB=tB, is_q=is_q):
        for f in range(8):
            i = v.tensor_tensor(t1b[:, f, :], qb[:, f, :], tabs[:, 0, :], OP.mult)
            self.inc("tV", i)
            v.wait_ge(self.sem["tT"], tB + 2 * f + 2)
            i = v.tensor_tensor(t2b[:, f, :], QB[:, (f % 2) * 512:(f % 2) * 512 + 512],
                                tabs[:, 1, :], OP.mult)
            self.edge(v, "tV", i)
            i = v.tensor_tensor(t1b[:, f, :], t1b[:, f, :], t2b[:, f, :], OP.add)
            self.edge(v, "tV", i)
            if is_q:
                for hh in range(2):
                    b0 = 64 * hh
                    i = v.tensor_tensor(qh[b0:b0 + 64, 2 * f + hh, :], t1b[b0:b0 + 64, f, :],
                                        PR[b0:b0 + 64, (f % 2) * 512:(f % 2) * 512 + 512], OP.mult)
                self.inc("tV", i)
            else:
                for j in range(4):
                    slot = (4 * it + j) % 6
                    for hh in range(2):
                        b0 = 64 * hh
                        i = v.tensor_tensor(kring[b0:b0 + 64, 2 * f + hh, slot, :],
                                            t1b[b0:b0 + 64, f, j * 128:j * 128 + 128],
                                            PR[b0:b0 + 64, (f % 2) * 512 + j * 128:(f % 2) * 512 + j * 128 + 128],
                                            OP.mult)
                self.inc("tV", i)
    self.block.vector(_)


def pb_ffn(self: Prog, li, srcn, dstn):
    ntile = self.cfg["n_tiles"]
    with contextlib.ExitStack() as pex:
        sb = lambda n, s, d=F32: self.sbuf(n + "_l%d" % li, s, d, ex=pex)
        w2s = sb("bf_w2", [128, 22, D], BF16)
        w1s = sb("bf_w1", [128, 2, 8, 256], BF16)
        xf = sb("bf_xf", [128, 8, TT + 1])
        xsq = sb("bf_xsq", [128, 8, TT + 1])
        rows = sb("bf_rows", [1, 6 * TT])
        xt1 = sb("bf_xt1", [128, 8, TT])
        xnbt = sb("bf_xnb", [128, 8, TT], BF16)
        gel = sb("bf_gel", [128, 2, TT])
        hbuf = sb("bf_h", [128, 22, TT], BF16)
        xfp = sb("bf_xfp", [128, 8, TT])
        QA = self.psum("bf_QA_l%d" % li, [128, 1024], ex=pex)
        QB = self.psum("bf_QB_l%d" % li, [128, 1024], ex=pex)
        PR = self.psum("bf_PR_l%d" % li, [128, 1024], ex=pex)
        PS = self.psum("bf_PS_l%d" % li, [128, 512], ex=pex)
        mu, mu2, s2s, var, rstd, mrs = (rows[:, i * TT:(i + 1) * TT] for i in range(6))
        ps1 = PS[0:1, 0:TT]
        ps2 = PS[0:1, TT - 1:TT - 1] if False else PS[0:1, 0:TT]
        ps1 = PR[0:1, 0:TT]
        ps2 = PR[0:1, TT:2 * TT]
        rbp = QB[:, 0:TT]
        mbp = QB[:, TT:2 * TT]

        def _(g, li=li):
            self.wait(g, "tT")
            i = g.dma_start(out=w2s[:, 0:21, :],
                            in_=self.dram["w_2"][li, 0:2688, :].rearrange("(k p) m -> p k m", p=128))
            self.inc("qG", i, 16)
            i = g.dma_start(out=w2s[0:42, 21, :], in_=self.dram["w_2"][li, 2688:2730, :])
            self.inc("qG", i, 16)
        self.block.gpsimd(_)

        for it in range(ntile):
            t0 = it * TT
            def _(q, t0=t0, srcn=srcn):
                self.wait(q, "tT")
                self.wait(q, "tS")
                self.wait(q, "tV")
                i = q.dma_start(out=xf[:, :, :], in_=self.dram[srcn][:, :, t0:t0 + TT + 1])
                self.inc("qS", i, 16)
            self.block.sync(_)
            def _(s):
                self.wait(s, "qS")
                i = s.activation(xsq[:, :, :], xf[:, :, :], AF.Square)
                self.inc("tS", i)
            self.block.scalar(_)
            def _(t):
                self.wait(t, "qS")
                self.wait(t, "tV")
                for f in range(8):
                    sl = (slice(1, 1 + TT) if f < 4 else slice(0, TT))
                    i = t.matmul(ps1, self.sb["onescol"], xf[:, f, sl], start=(f == 0), stop=(f == 7))
                self.inc("tT", i)
            self.block.tensor(_)
            def _(t):
                self.wait(t, "tS")
                for f in range(8):
                    sl = (slice(1, 1 + TT) if f < 4 else slice(0, TT))
                    i = t.matmul(ps2, self.sb["onescol"], xsq[:, f, sl], start=(f == 0), stop=(f == 7))
                self.inc("tT", i)
            self.block.tensor(_)
            def _(s):
                self.wait(s, "tT")
                self.edge(s, "tS", s.activation(mu, ps1, AF.Copy, scale=self.const("inv1024", True)))
                s.activation(s2s, ps2, AF.Copy, scale=self.const("inv1024", True))
                i = s.activation(mu2, mu, AF.Square)
                self.inc("tS", i)
            self.block.scalar(_)
            def _(v):
                self.wait(v, "tS")
                i = v.tensor_tensor(var, s2s, mu2, OP.subtract)
                self.inc("tV", i)
            self.block.vector(_)
            def _(s):
                self.wait(s, "tV")
                self.edge(s, "tS", s.activation(var, var, AF.Ln, bias=self.const("lneps", True)))
                i = s.activation(rstd, var, AF.Exp, scale=self.const("neghalf", True))
                self.inc("tS", i)
            self.block.scalar(_)
            def _(v):
                self.wait(v, "tS")
                i = v.tensor_tensor(mrs, mu, rstd, OP.mult)
                self.inc("tV", i)
            self.block.vector(_)
            def _(t):
                self.wait(t, "tV")
                t.matmul(rbp, self.sb["onesrow"], rstd, start=True, stop=True)
                i = t.matmul(mbp, self.sb["onesrow"], mrs, start=True, stop=True)
                self.inc("tT", i)
            self.block.tensor(_)
            def _(v):
                self.wait(v, "tT")
                for f in range(8):
                    sl = (slice(1, 1 + TT) if f < 4 else slice(0, TT))
                    i = v.tensor_tensor(xt1[:, f, :], xf[:, f, sl], rbp, OP.mult)
                self.edge(v, "tV", i)
                for f in range(8):
                    i = v.tensor_tensor(xt1[:, f, :], xt1[:, f, :], mbp, OP.subtract)
                self.inc("tV", i)
            self.block.vector(_)
            def _(v, li=li):
                self.wait(v, "tV")
                for f in range(8):
                    i = v.tensor_scalar(xnbt[:, f, :], xt1[:, f, :],
                                        self.bankcol("lng", li * 8 + f),
                                        self.bankcol("lnb", li * 8 + f), OP.mult, OP.add)
                    self.inc("tV", i)
            self.block.vector(_)

            gB = self.cnt["qG"]
            tB = self.cnt["tT"]
            sB = self.cnt["tS"]
            vB = self.cnt["tV"]
            w1aB = self.cnt["w1a"]
            w1bB = self.cnt["w1b"]
            w1v = self.dram["w_1"][li].rearrange("(k p) m -> p k m", p=128)
            def _(g, li=li, gB=gB, tB=tB):
                self.wait(g, "tT")
                self.wait(g, "tS")
                self.wait(g, "tV")
                for j in range(22):
                    na = min(128, DFF - j * 128)
                    if j >= 2:
                        g.wait_ge(self.sem["tT"], tB + 2 * (j - 1))
                    wsem = "w1a" if j % 2 == 0 else "w1b"
                    i = g.dma_start(out=w1s[:, j % 2, :, 0:na],
                                    in_=w1v[:, :, j * 128:j * 128 + na])
                    self.inc(wsem, i, 16)
                    i = g.dma_start(out=w1s[:, j % 2, :, 128:128 + na],
                                    in_=w1v[:, :, DFF + j * 128:DFF + j * 128 + na])
                    self.inc(wsem, i, 16)
            self.block.gpsimd(_)
            def _(t, gB=gB, vB=vB, w1aB=w1aB, w1bB=w1bB):
                self.wait(t, "tS")
                self.wait(t, "tV")
                for j in range(22):
                    na = min(128, DFF - j * 128)
                    PPX = QA if j % 2 == 0 else QB
                    if j % 2 == 0:
                        t.wait_ge(self.sem["w1a"], w1aB + 32 * (j // 2 + 1))
                    else:
                        t.wait_ge(self.sem["w1b"], w1bB + 32 * (j // 2 + 1))
                    if j >= 2:
                        t.wait_ge(self.sem["tV"], vB + j - 1)
                    for kb in range(8):
                        i = t.matmul(PPX[0:na, 0:512], w1s[:, j % 2, kb, 0:na], xnbt[:, kb, :],
                                     start=(kb == 0), stop=(kb == 7))
                    self.inc("tT", i)
                    for kb in range(8):
                        i = t.matmul(PPX[0:na, 512:1024], w1s[:, j % 2, kb, 128:128 + na], xnbt[:, kb, :],
                                     start=(kb == 0), stop=(kb == 7))
                    self.inc("tT", i)
            self.block.tensor(_)
            def _(s, li=li, tB=tB, vB=vB):
                for j in range(22):
                    na = min(128, DFF - j * 128)
                    PPX = QA if j % 2 == 0 else QB
                    s.wait_ge(self.sem["tT"], tB + 2 * (j + 1))
                    if j >= 2:
                        s.wait_ge(self.sem["tV"], vB + j - 1)
                    i = s.activation(gel[0:na, j % 2, :], PPX[0:na, 512:1024], AF.Gelu,
                                     bias=self.bankcol("b1g", li * 22 + j)[0:na, :])
                    self.inc("tS", i)
            self.block.scalar(_)
            def _(v, li=li, tB=tB, sB=sB):
                for j in range(22):
                    na = min(128, DFF - j * 128)
                    PPX = QA if j % 2 == 0 else QB
                    v.wait_ge(self.sem["tS"], sB + j + 1)
                    i = v.scalar_tensor_tensor(hbuf[0:na, j, :], PPX[0:na, 0:512],
                                               self.bankcol("b1a", li * 22 + j)[0:na, :],
                                               gel[0:na, j % 2, :], OP.add, OP.mult)
                    self.inc("tV", i)
            self.block.vector(_)

            tB = self.cnt["tT"]
            vB = self.cnt["tV"]
            def _(t, tB=tB, vB=vB):
                self.wait(t, "tV")
                self.wait(t, "qG")
                for mt in range(8):
                    if mt >= 2:
                        t.wait_ge(self.sem["tV"], vB + mt - 1)
                    pp = QB[:, (mt % 2) * 512:(mt % 2) * 512 + 512]
                    for kb in range(22):
                        na = min(128, DFF - kb * 128)
                        i = t.matmul(pp, w2s[0:na, kb, mt * 128:mt * 128 + 128], hbuf[0:na, kb, :],
                                     start=(kb == 0), stop=(kb == 21))
                    self.inc("tT", i)
            self.block.tensor(_)
            def _(v, li=li, tB=tB):
                for mt in range(8):
                    v.wait_ge(self.sem["tT"], tB + mt + 1)
                    pp = QB[:, (mt % 2) * 512:(mt % 2) * 512 + 512]
                    i = v.scalar_tensor_tensor(xfp[:, mt, :], pp, self.bankcol("b2", li * 8 + mt),
                                               xf[:, mt, 1:1 + TT], OP.add, OP.add)
                    self.inc("tV", i)
            self.block.vector(_)
            def _(q, t0=t0, dstn=dstn):
                self.wait(q, "tV")
                i = q.dma_start(out=self.dram[dstn][:, :, 1 + t0:1 + t0 + TT], in_=xfp[:, :, :])
                self.inc("qS", i, 16)
            self.block.sync(_)
        self.phase_barrier()

Prog.pb_attn = pb_attn
Prog.pb_ffn = pb_ffn
Prog.pb_qk_section = pb_qk_section

def pc(self: Prog):
    nch = self.cfg["n_chunks"]
    with contextlib.ExitStack() as pex:
        sb = lambda n, s, d=F32: self.sbuf(n, s, d, ex=pex)
        l2m = {}
        for nm, q in (("f", QCH), ("b", QCH), ("fs", QLAST), ("bs", QLAST)):
            l2m["L2" + nm] = sb("pc_L2" + nm, [q, q + 1])
            l2m["M" + nm] = sb("pc_M" + nm, [q + 1, q], BF16)
        negAT = sb("pc_negAT", [DS, DI], BF16)
        arow = sb("pc_arow", [1, 1024])
        negab = sb("pc_negab", [128, DS, 1024], BF16)
        wbuf = sb("pc_wbuf", [128, DS, 1024], BF16)
        rp = sb("pc_rp", [128, DS, 1024], BF16)
        hbuf = sb("pc_hbuf", [128, DS, 1024], BF16)
        etmp = sb("pc_etmp", [128, 2, 1024])
        rmt = sb("pc_rmt", [128, 2, 1024])
        cdq = sb("pc_cdq", [128, 1024])
        dtt = sb("pc_dtt", [128, 2, 1024])
        utt = sb("pc_utt", [128, 2, 1024], BF16)
        bctt = sb("pc_bctt", [128, 2, 32])
        dgc = sb("pc_dgc", [128, DS, QCH], BF16)
        mstage = sb("pc_mstage", [1, 1024], BF16)
        ea = sb("pc_ea", [DS, 1024])
        gg = sb("pc_gg", [DS, 1024], BF16)
        wc = sb("pc_wc", [DS, 1024], BF16)
        hprev = sb("pc_hprev", [DS, 1024], BF16)
        ys = sb("pc_ys", [128, 2, 1024])
        QA = self.psum("pc_QA", [128, 1024], ex=pex)
        QB = self.psum("pc_QB", [128, 1024], ex=pex)
        QY = self.psum("pc_QY", [128, 1024], ex=pex)
        MC = self.psum("pc_MC", [128, 1024], ex=pex)
        MB = MC[0:DS, :]

        def _(v):
            self.wait(v, "tT")
            self.wait(v, "tS")
            v.memset(dtt[:, :], 0.0)
            v.memset(utt[:, :], 0.0)
            v.memset(bctt[:, :], 0.0)
            i = v.memset(cdq[:, :], 0.0)
            self.inc("tV", i)
        self.block.vector(_)
        def _(g):
            self.wait(g, "tT")
            self.wait(g, "tS")
            self.wait(g, "tV")
            for nm in ("L2f", "L2b", "L2fs", "L2bs"):
                i = g.dma_start(out=l2m[nm], in_=self.dram[nm])
                self.inc("qG", i, 16)
            for nm in ("Mf", "Mb", "Mfs", "Mbs"):
                i = g.dma_start(out=l2m[nm], in_=self.dram[nm])
                self.inc("qG", i, 16)
        self.block.gpsimd(_)

        marks = {"prev": None}
        for dr in self.cfg["dirs"]:
            for hh in range(2):
                def _(g, dr=dr, hh=hh):
                    self.wait(g, "tS")
                    self.wait(g, "tV")
                    i = g.dma_start(out=ea[:, :], in_=self.dram["alogT"][dr, :, hh * 1024:hh * 1024 + 1024])
                    self.inc("qG", i, 16)
                self.block.gpsimd(_)
                def _(s, dr=dr, hh=hh):
                    self.wait(s, "qG")
                    i = s.activation(negAT[:, hh * 1024:hh * 1024 + 1024], ea[:, :], AF.Exp)
                    self.inc("tS", i)
                self.block.scalar(_)

            for hf in range(2):
                d0 = hf * 1024
                for s_ in range(DS):
                    def _(g, dr=dr, s_=s_, d0=d0):
                        self.wait(g, "tS")
                        self.wait(g, "tV")
                        i = g.dma_start(out=arow[:, :], in_=self.dram["alogT"][dr, s_:s_ + 1, d0:d0 + 1024])
                        self.inc("qG", i, 16)
                    self.block.gpsimd(_)
                    def _(s, s_=s_):
                        self.wait(s, "qG")
                        i = s.activation(mstage[:, :], arow[:, :], AF.Exp)
                        self.inc("tS", i)
                    self.block.scalar(_)
                    def _(t, s_=s_):
                        self.wait(t, "tS")
                        t.matmul(QA[:, 0:512], self.sb["onesrow_b"][0:1, :], mstage[:, 0:512], start=True, stop=True)
                        i = t.matmul(QA[:, 512:1024], self.sb["onesrow_b"][0:1, :], mstage[:, 512:1024], start=True, stop=True)
                        self.inc("tT", i)
                    self.block.tensor(_)
                    def _(s, s_=s_):
                        self.wait(s, "tT")
                        i = s.copy(negab[:, s_, :], QA)
                        self.inc("tS", i)
                    self.block.scalar(_)
                def _(v, dr=dr, hf=hf):
                    self.wait(v, "tS")
                    self.wait(v, "qG")
                    i = v.memset(hprev[:, :], 0.0)
                    self.inc("tV", i)
                self.block.vector(_)

                if dr == 0:
                    chunks = [(c, QCH) for c in range(nch - 1)] + [(nch - 1, QLAST)]
                    mats = ("L2f", "Mf", "L2fs", "Mfs")
                else:
                    chunks = [(nch - 1, QLAST)] + [(c, QCH) for c in range(nch - 2, -1, -1)]
                    mats = ("L2b", "Mb", "L2bs", "Mbs")

                marks["prev"] = None
                nch_l = len(chunks)
                qneed = {}

                def emit_loads(ci, waitm, dr=dr, d0=d0):
                    c, Q = chunks[ci]
                    c0 = c * QCH
                    cp = ci % 2
                    psem = "pcA" if cp == 0 else "pcB"
                    def _(q, ci=ci, c0=c0, Q=Q, cp=cp, psem=psem, waitm=waitm):
                        if waitm is not None:
                            q.wait_ge(self.sem["tT"], waitm[0])
                            q.wait_ge(self.sem["tS"], waitm[1])
                            q.wait_ge(self.sem["tV"], waitm[2])
                        else:
                            self.wait(q, "tT")
                            self.wait(q, "tV")
                            self.wait(q, "tS")
                        i = q.dma_start(out=dtt[0:Q, cp, :], in_=self.dram["dtT_sc"][dr, c0:c0 + Q, d0:d0 + 1024])
                        self.inc(psem, i, 16)
                        i = q.dma_start(out=utt[0:Q, cp, :], in_=self.dram["uT_sc"][dr, c0:c0 + Q, d0:d0 + 1024])
                        self.inc(psem, i, 16)
                        i = q.dma_start(out=bctt[0:Q, cp, :], in_=self.dram["bct_sc"][dr, c0:c0 + Q, :])
                        self.inc(psem, i, 16)
                    self.block.sync(_)
                    qneed[ci] = self.cnt[psem]

                emit_loads(0, None)
                emit_loads(1, None)
                for ci, (c, Q) in enumerate(chunks):
                    c0 = c * QCH
                    short = (Q != QCH)
                    L2 = l2m[mats[2] if short else mats[0]]
                    Mh = l2m[mats[3] if short else mats[1]]
                    rowe = (Q - 1) if dr == 0 else 0
                    cp = ci % 2
                    psem = "pcA" if cp == 0 else "pcB"
                    qs_need = qneed[ci]
                    prev = marks.get("prev")
                    Vord = [("EA", -1), ("E", 0), ("E", 1), ("stt", 0), ("E", 2), ("stt", 1), ("wc", -1)]
                    for s_ in range(3, DS):
                        Vord += [("Hp", s_ - 3), ("E", s_), ("stt", s_ - 1)]
                    Vord += [("Hp", DS - 3), ("stt", DS - 1), ("Hp", DS - 2), ("Hp", DS - 1)]
                    VP = {op: k + 1 for k, op in enumerate(Vord)}

                    def _(t, Q=Q, L2=L2, cp=cp, qs_need=qs_need, psem=psem, prev=prev):
                        t.wait_ge(self.sem[psem], qs_need)
                        if prev:
                            t.wait_ge(self.sem["tS"], prev["sEvac"])
                            t.wait_ge(self.sem["tV"], prev["vB"] + 1)
                        else:
                            self.wait(t, "tS")
                            self.wait(t, "tV")
                        t.matmul(MC[0:Q + 1, 0:512], L2[0:Q, :], dtt[0:Q, cp, 0:512], start=True, stop=True)
                        i = t.matmul(MC[0:Q + 1, 512:1024], L2[0:Q, :], dtt[0:Q, cp, 512:1024], start=True, stop=True)
                        self.inc("tT", i)
                    self.block.tensor(_)
                    t1pos = self.cnt["tT"]
                    def _(s, Q=Q, cp=cp, qs_need=qs_need, psem=psem, prev=prev, t1pos=t1pos):
                        s.wait_ge(self.sem["tT"], t1pos)
                        if prev:
                            s.wait_ge(self.sem["tV"], prev["vB"] + prev["VP"][("E", DS - 1)])
                            s.wait_ge(self.sem["qG"], prev["qg_m"])
                        else:
                            self.wait(s, "tV")
                            self.wait(s, "qG")
                        i = s.copy(cdq[0:Q + 1, :], MC[0:Q + 1, :])
                        self.inc("tS", i)
                    self.block.scalar(_)
                    sEvac = self.cnt["tS"]
                    def _(g, Q=Q, sEvac=sEvac):
                        g.wait_ge(self.sem["tS"], sEvac)
                        i = g.dma_start(out=mstage[:, :], in_=cdq[Q:Q + 1, :])
                        self.inc("qG", i, 16)
                    self.block.gpsimd(_)
                    qg_m = self.cnt["qG"]
                    def _(t, qg_m=qg_m, sEvac=sEvac):
                        t.wait_ge(self.sem["qG"], qg_m)
                        t.wait_ge(self.sem["tS"], sEvac)
                        t.matmul(MB[:, 0:512], self.sb["onesrow_b"][0:1, 0:DS], mstage[:, 0:512], start=True, stop=True)
                        i = t.matmul(MB[:, 512:1024], self.sb["onesrow_b"][0:1, 0:DS], mstage[:, 512:1024], start=True, stop=True)
                        self.inc("tT", i)
                    self.block.tensor(_)
                    t2pos = self.cnt["tT"]
                    vB = self.cnt["tV"]
                    def _(s, vB=vB):
                        s.wait_ge(self.sem["tV"], vB + 1)
                        i = s.activation(gg[:, :], ea[:, :], AF.Exp)
                        self.inc("tS", i)
                    self.block.scalar(_)
                    sB = self.cnt["tS"]
                    def _(g, Q=Q, vB=vB, VP=VP):
                        g.wait_ge(self.sem["tV"], vB + VP[("wc", -1)])
                        i = g.dma_start(out=wbuf[Q:Q + 1, :, :], in_=wc[:, :])
                        self.inc("qG", i, 16)
                    self.block.gpsimd(_)
                    qg_c = self.cnt["qG"]

                    tB3 = self.cnt["tT"]
                    tpos_P = {}
                    tpos_y = {}
                    tk = 0
                    for s_ in range(DS):
                        tk += 1
                        tpos_P[s_] = tB3 + tk
                        if s_ >= 1:
                            tk += 1
                            tpos_y[s_ - 1] = tB3 + tk
                    tk += 1
                    t3end = tB3 + tk
                    tpos_y[DS - 1] = t3end

                    def _(v, Q=Q, cp=cp, d0=d0, vB=vB, sB=sB, Vord=Vord, tpos_P=tpos_P, t2pos=t2pos):
                        for op, s_ in Vord:
                            par = s_ % 2
                            if op == "EA":
                                v.wait_ge(self.sem["tT"], t2pos)
                                i = v.tensor_tensor(ea[:, :], negAT[:, d0:d0 + 1024], MB, OP.mult)
                            elif op == "wc":
                                v.wait_ge(self.sem["tS"], sB)
                                i = v.tensor_tensor(wc[:, :], hprev[:, :], gg[:, :], OP.mult)
                            elif op == "E":
                                if s_ >= 2:
                                    v.wait_ge(self.sem["tS"], sB + 3 * s_ - 4)
                                i = v.tensor_tensor(etmp[:, par, :], cdq[:, :], negab[:, s_, :], OP.mult)
                            elif op == "stt":
                                v.wait_ge(self.sem["tS"], sB + 3 * s_ + 1)
                                i = v.scalar_tensor_tensor(wbuf[0:Q, s_, :], rmt[0:Q, par, :],
                                                           bctt[0:Q, cp, s_:s_ + 1], utt[0:Q, cp, :], OP.mult, OP.mult)
                            else:
                                v.wait_ge(self.sem["tT"], tpos_P[s_])
                                if s_ == DS - 1:
                                    v.wait_ge(self.sem["tS"], sB + 3 * s_ + 2)
                                pp = QA if s_ % 2 == 0 else QB
                                i = v.tensor_tensor(hbuf[0:Q, s_, :], rp[0:Q, s_, :], pp[0:Q, :], OP.mult)
                            self.inc("tV", i)
                    self.block.vector(_)
                    def _(s, Q=Q, cp=cp, vB=vB, VP=VP, prev=prev):
                        for s_ in range(DS):
                            par = s_ % 2
                            s.wait_ge(self.sem["tV"], vB + VP[("E", s_)])
                            if s_ >= 2:
                                s.wait_ge(self.sem["tV"], vB + VP[("stt", s_ - 2)])
                            i = s.activation(rmt[:, par, :], etmp[:, par, :], AF.Exp)
                            self.inc("tS", i)
                            i = s.activation(rp[:, s_, :], etmp[:, par, :], AF.Exp, scale=self.const("negone"))
                            self.inc("tS", i)
                            if prev:
                                s.wait_ge(self.sem["tT"], prev["tpos_y"][s_])
                            i = s.activation(dgc[0:Q, s_, 0:Q], self.sb["idf32"][0:Q, 0:Q], AF.Copy,
                                             scale=bctt[0:Q, cp, 16 + s_:17 + s_])
                            self.inc("tS", i)
                    self.block.scalar(_)
                    def _(t, Q=Q, Mh=Mh, vB=vB, sB=sB, VP=VP, qg_c=qg_c, prev=prev):
                        t.wait_ge(self.sem["qG"], qg_c)
                        for s_ in range(DS):
                            t.wait_ge(self.sem["tV"], vB + VP[("stt", s_)])
                            if s_ >= 2:
                                t.wait_ge(self.sem["tV"], vB + VP[("Hp", s_ - 2)])
                            pp = QA if s_ % 2 == 0 else QB
                            t.matmul(pp[0:Q, 0:512], Mh[0:Q + 1, 0:Q], wbuf[0:Q + 1, s_, 0:512],
                                     start=True, stop=True)
                            i = t.matmul(pp[0:Q, 512:1024], Mh[0:Q + 1, 0:Q], wbuf[0:Q + 1, s_, 512:1024],
                                         start=True, stop=True)
                            self.inc("tT", i)
                            if s_ >= 1:
                                t.wait_ge(self.sem["tV"], vB + VP[("Hp", s_ - 1)])
                                t.wait_ge(self.sem["tS"], sB + 3 * s_)
                                if s_ == 1 and prev:
                                    t.wait_ge(self.sem["tS"], prev["s4pos"])
                                t.matmul(QY[0:Q, 0:512], dgc[0:Q, s_ - 1, 0:Q], hbuf[0:Q, s_ - 1, 0:512],
                                         start=(s_ == 1), stop=False)
                                i = t.matmul(QY[0:Q, 512:1024], dgc[0:Q, s_ - 1, 0:Q], hbuf[0:Q, s_ - 1, 512:1024],
                                             start=(s_ == 1), stop=False)
                                self.inc("tT", i)
                        t.wait_ge(self.sem["tV"], vB + VP[("Hp", DS - 1)])
                        t.wait_ge(self.sem["tS"], sB + 3 * DS)
                        t.matmul(QY[0:Q, 0:512], dgc[0:Q, DS - 1, 0:Q], hbuf[0:Q, DS - 1, 0:512],
                                 start=False, stop=True)
                        i = t.matmul(QY[0:Q, 512:1024], dgc[0:Q, DS - 1, 0:Q], hbuf[0:Q, DS - 1, 512:1024],
                                     start=False, stop=True)
                        self.inc("tT", i)
                    self.block.tensor(_)
                    assert self.cnt["tT"] == t3end, (self.cnt["tT"], t3end)
                    def _(g, rowe=rowe, vB=vB, VP=VP):
                        g.wait_ge(self.sem["tV"], vB + VP[("Hp", DS - 1)])
                        i = g.dma_start(out=hprev[:, :], in_=hbuf[rowe:rowe + 1, :, :])
                        self.inc("qG", i, 16)
                    self.block.gpsimd(_)
                    ysem = "ysA" if cp == 0 else "ysB"
                    st_prev = marks.get("st_" + ysem, 0)
                    def _(s, Q=Q, cp=cp, t3end=t3end, st_prev=st_prev, ysem=ysem):
                        s.wait_ge(self.sem["tT"], t3end)
                        if st_prev:
                            s.wait_ge(self.sem[ysem], st_prev)
                        i = s.copy(ys[0:Q, cp, :], QY[0:Q, :])
                        self.inc("tS", i)
                    self.block.scalar(_)
                    s4pos = self.cnt["tS"]
                    def _(q, dr=dr, c0=c0, Q=Q, cp=cp, d0=d0, s4pos=s4pos, ysem=ysem):
                        q.wait_ge(self.sem["tS"], s4pos)
                        i = q.dma_start(out=self.dram["yTok_sc"][dr, c0:c0 + Q, d0:d0 + 1024],
                                        in_=ys[0:Q, cp, :])
                        self.inc(ysem, i, 16)
                    self.block.sync(_)
                    marks["st_" + ysem] = self.cnt[ysem]
                    marks["prev"] = dict(sEvac=sEvac, vB=vB, VP=VP, qg_m=qg_m, s4pos=s4pos, tpos_y=tpos_y)
                    marks[ci] = (self.cnt["tT"], self.cnt["tS"], self.cnt["tV"])
                    if ci + 2 < nch_l:
                        emit_loads(ci + 2, marks[ci])
        self.phase_barrier()

def pd(self: Prog):
    ntile = self.cfg["n_tiles"]
    with contextlib.ExitStack() as pex:
        sb = lambda n, s, d=F32: self.sbuf(n, s, d, ex=pex)
        ytt = sb("pd_ytt", [128, 16, TT])
        ytok = sb("pd_ytok", [128, 2, DI])
        xct = sb("pd_xct", [128, 16, TT], BF16)
        szt = sb("pd_szt", [128, 16, TT])
        dpx = sb("pd_dpx", [128, 16, TT])
        yft = sb("pd_yft", [128, 16, TT])
        ygt = sb("pd_ygt", [128, 16, TT], BF16)
        PT = self.psum("pd_PT", [128, 2 * DI], ex=pex)
        for dr in self.cfg["dirs"]:
            for it in range(ntile):
                t0 = it * TT
                def _(q, dr=dr, t0=t0):
                    self.wait(q, "tS")
                    self.wait(q, "tV")
                    i = q.dma_start(out=xct[:, :, :], in_=self.dram["xcb_sc"][dr, :, :, t0:t0 + TT])
                    self.inc("qS", i, 16)
                    i = q.dma_start(out=szt[:, :, :], in_=self.dram["szf_sc"][dr, :, :, t0:t0 + TT])
                    self.inc("qS", i, 16)
                self.block.sync(_)
                qB = self.cnt["qS"]
                tB = self.cnt["tT"]
                vB = self.cnt["tV"]
                for b in range(4):
                    bp = b % 2
                    def _(q, dr=dr, t0=t0, b=b, bp=bp, vB=vB):
                        if b >= 2:
                            q.wait_ge(self.sem["tV"], vB + b - 1)
                        elif b == 0:
                            self.wait(q, "tV")
                        i = q.dma_start(out=ytok[:, bp, :],
                                        in_=self.dram["yTok_sc"][dr, t0 + b * 128:t0 + (b + 1) * 128, :])
                        self.inc("qS", i, 16)
                    self.block.sync(_)
                    def _(t, b=b, bp=bp, qB=qB, vB=vB):
                        q_need = qB + 16 * (b + 1)
                        t.wait_ge(self.sem["qS"], q_need)
                        if b >= 2:
                            t.wait_ge(self.sem["tV"], vB + b - 1)
                        for k in range(16):
                            i = t.transpose(PT[:, bp * DI + k * 128:bp * DI + (k + 1) * 128],
                                            ytok[:, bp, k * 128:(k + 1) * 128],
                                            self.sb["idf32"])
                        self.inc("tT", i)
                    self.block.tensor(_)
                    def _(v, b=b, bp=bp, tB=tB):
                        v.wait_ge(self.sem["tT"], tB + b + 1)
                        i = v.tensor_scalar(ytt[:, :, b * 128:b * 128 + 128],
                                            PT[:, bp * DI:(bp + 1) * DI].rearrange("p (k x) -> p k x", k=16),
                                            1.0, None, OP.mult)
                        self.inc("tV", i)
                    self.block.vector(_)
                def _(s, dr=dr):
                    self.wait(s, "qS")
                    self.wait(s, "tV")
                    for b in range(16):
                        i = s.activation(dpx[:, b, :], xct[:, b, :], AF.Copy,
                                         scale=self.bankcol("dp", dr * 16 + b))
                    self.inc("tS", i)
                self.block.scalar(_)
                def _(v, dr=dr):
                    self.wait(v, "tS")
                    self.wait(v, "qS")
                    i = v.tensor_tensor(yft[:, :, :], ytt[:, :, :], dpx[:, :, :], OP.add)
                    self.edge(v, "tV", i)
                    i = v.tensor_tensor(ygt[:, :, :], yft[:, :, :], szt[:, :, :], OP.mult)
                    self.inc("tV", i)
                self.block.vector(_)
                def _(q, dr=dr, t0=t0):
                    self.wait(q, "tV")
                    i = q.dma_start(out=self.dram["yg_sc"][dr, :, :, t0:t0 + TT], in_=ygt[:, :, :])
                    self.inc("qS", i, 16)
                self.block.sync(_)
        self.phase_barrier()

    with contextlib.ExitStack() as pex:
        sb = lambda n, s, d=F32: self.sbuf(n, s, d, ex=pex)
        w_op = sb("pd_wop", [128, 2, 16, D], BF16)
        ygf = sb("pd_ygf", [128, 16, TT], BF16)
        ygb = sb("pd_ygb", [128, 16, TT], BF16)
        xft = sb("pd_xft", [128, 8, TT])
        xtt = sb("pd_xtt", [128, 8, TT])
        t1 = sb("pd_t1", [128, TT])
        ost = sb("pd_ost", [128, 8, TT])
        PP = [self.psum("pd_P0", [128, 512], ex=pex), self.psum("pd_P1", [128, 512], ex=pex)]
        def _(g):
            self.wait(g, "tT")
            for dr in self.cfg["dirs"]:
                i = g.dma_start(out=w_op[:, dr, :, :], in_=self.dram["w_outproj"][dr].rearrange("(k p) m -> p k m", p=128))
                self.inc("qG", i, 16)
        self.block.gpsimd(_)
        xTv = self.dram["xT"].rearrange("(f p) t -> p f t", p=128)
        for it in range(ntile):
            t0 = it * TT
            def _(q, t0=t0):
                self.wait(q, "tT")
                self.wait(q, "tV")
                self.wait(q, "tS")
                i = q.dma_start(out=ygf[:, :, :], in_=self.dram["yg_sc"][0, :, :, t0:t0 + TT])
                self.inc("qS", i, 16)
                if len(self.cfg["dirs"]) > 1:
                    i = q.dma_start(out=ygb[:, :, :], in_=self.dram["yg_sc"][1, :, :, t0:t0 + TT])
                    self.inc("qS", i, 16)
                i = q.dma_start(out=xft[:, :, :], in_=self.dram["xfA"][:, :, 1 + t0:1 + t0 + TT])
                self.inc("qS", i, 16)
                i = q.dma_start(out=xtt[:, :, :], in_=xTv[:, :, t0:t0 + TT])
                self.inc("qS", i, 16)
            self.block.sync(_)
            tB = self.cnt["tT"]
            sB = self.cnt["tV"]
            def _(t, tB=tB, sB=sB):
                self.wait(t, "qS")
                self.wait(t, "qG")
                self.wait(t, "tV")
                for mt in range(8):
                    if mt >= 2:
                        t.wait_ge(self.sem["tV"], sB + 2 * (mt - 1))
                    pp = PP[mt % 2]
                    first = True
                    for dr in self.cfg["dirs"]:
                        yg = ygf if dr == 0 else ygb
                        for kb in range(16):
                            i = t.matmul(pp, w_op[:, dr, kb, mt * 128:mt * 128 + 128], yg[:, kb, :],
                                         start=first, stop=(dr == self.cfg["dirs"][-1] and kb == 15))
                            first = False
                    self.inc("tT", i)
            self.block.tensor(_)
            def _(v, tB=tB):
                for mt in range(8):
                    v.wait_ge(self.sem["tT"], tB + mt + 1)
                    pp = PP[mt % 2]
                    i = v.tensor_tensor(t1[:, :], pp, xft[:, mt, :], OP.add)
                    self.edge(v, "tV", i)
                    i = v.tensor_tensor(ost[:, mt, :], t1[:, :], xtt[:, mt, :], OP.add)
                    self.edge(v, "tV", i)
            self.block.vector(_)
            def _(q, t0=t0):
                self.wait(q, "tV")
                i = q.dma_start(out=self.dram["outT"].rearrange("(f p) t -> p f t", p=128)[:, :, t0:t0 + TT],
                                in_=ost[:, :, :])
                self.inc("qS", i, 16)
            self.block.sync(_)
        self.phase_barrier()

Prog.pb_main = pb_main
Prog.pc = pc
Prog.pd = pd



def kernel(**inputs):
    consts = _host_consts()
    w = _host_weights(inputs)
    x = np.asarray(inputs["L0_attn_x"])
    prog = Prog(CFG)
    nc = prog.build()
    shared = {}
    for k, v in list(w.items()) + list(consts.items()):
        shared[k] = np.ascontiguousarray(v.astype(np.float32))
    in_maps = []
    for c in range(B):
        m = dict(shared)
        m["xT"] = np.ascontiguousarray(x[c].T)
        in_maps.append(m)
    res = run_bass_kernel_spmd(nc, in_maps, core_ids=list(range(B)))
    out = np.empty((B, T, D), dtype=np.float32)
    for c in range(B):
        out[c] = res.results[c]["outT"].T
    return out
